# version 02
# baseline (speedup 1.0000x reference)
import numpy as np
from contextlib import ExitStack

import concourse.bass as bass
import concourse.mybir as mybir
from concourse.bass_utils import run_bass_kernel_spmd

F32 = mybir.dt.float32
BF16 = mybir.dt.bfloat16
I32 = mybir.dt.int32
AF = mybir.ActivationFunctionType
ALU = mybir.AluOpType
AXX = mybir.AxisListType.X

D = 1024
KC = 8
T = 2048
TK = 4096
TT = 512
P = 128
LN_EPS = 1e-5
ENGS = ("t", "v", "s", "g", "y")


class Seq:
    def __init__(self, nc, stack):
        self.nc = nc
        self.st = stack
        self.blk = stack.enter_context(nc.Block())
        self.sem = {e: stack.enter_context(nc.semaphore(f"sq_{e}")) for e in ENGS}
        self.cnt = {e: 0 for e in ENGS}
        self.prev = None
        self._attr = {"t": "tensor", "v": "vector", "s": "scalar", "g": "gpsimd", "y": "sync"}

    def sb(self, name, p, f, dt):
        return self.st.enter_context(self.nc.sbuf_tensor(name, [p, f], dt))

    def ps(self, name, p, f, dt=F32):
        return self.st.enter_context(self.nc.psum_tensor(name, [p, f], dt))

    def run(self, eng, fn, after="prev", inc=1):
        if after == "prev":
            deps = [self.prev] if self.prev else []
        elif after is None:
            deps = []
        elif isinstance(after, list):
            deps = [a for a in after if a]
        else:
            deps = [after]
        section = getattr(self.blk, self._attr[eng])
        holder = {}

        def body(h):
            for (se, sv) in deps:
                h.wait_ge(self.sem[se], sv)
            out = fn(h)
            instrs = out if isinstance(out, (list, tuple)) else [out]
            instrs = [i for i in instrs if i is not None]
            if not instrs:
                instrs = [h.engine_nop()]
            for ins in instrs:
                self.cnt[eng] += inc
                ins.then_inc(self.sem[eng], inc)
            holder["tok"] = (eng, self.cnt[eng])
        section(body)
        self.prev = holder["tok"]
        return self.prev


def dma_step(q, eng, pairs, after="prev"):
    def fn(h):
        return [h.dma_start(out=o, in_=i) for (o, i) in pairs]
    return q.run(eng, fn, after=after, inc=16)


def chunks8(w):
    return np.ascontiguousarray(np.asarray(w, np.float32).reshape(KC, P, -1))


class Common:
    def __init__(self, q):
        self.xs = q.sb("c_xs", P, KC * TT, F32)
        self.xbs = q.sb("c_xbs", P, KC * TT, BF16)
        self.sq = q.sb("c_sq", P, KC * TT, BF16)
        self.rows = q.sb("c_rows", 1, 2 * TK, F32)
        self.ones = q.sb("c_ones", P, 8, F32)
        self.onesr = q.sb("c_onesr", 1, TT, F32)
        self.onesb = q.sb("c_onesb", P, 2, BF16)
        self.onesrb = q.sb("c_onesrb", 1, TT, BF16)
        self.bias = q.sb("c_bias", 1, 4352, F32)
        self.wf = q.sb("c_wf", P, 2 * D, F32)
        self.ot = q.sb("c_ot", P, KC * TT, BF16)
        self.pA = q.ps("c_pA", P, TT)
        self.pB = q.ps("c_pB", P, TT)
        self.pC = q.ps("c_pC", P, TT)
        self.pD = q.ps("c_pD", P, TT)
        q.run("v", lambda v: [v.memset(self.ones[:], 1.0),
                              v.memset(self.onesr[:], 1.0),
                              v.memset(self.onesb[:], 1.0),
                              v.memset(self.onesrb[:], 1.0)], after=None)


def ln_to_dram(q, cm, x_h, xn_d, ncols):
    nt = ncols // TT
    for j in range(nt):
        tload = dma_step(q, "y", [(cm.xs[:, c * TT:(c + 1) * TT],
                                   x_h[c * P:(c + 1) * P, j * TT:(j + 1) * TT])
                                  for c in range(KC)])
        q.run("s", lambda s: s.activation(cm.sq[:], cm.xs[:], AF.Square))

        def mms(t):
            for c in range(KC):
                t.matmul(cm.pA[0:1, :], cm.ones[:, 0:1], cm.xs[:, c * TT:(c + 1) * TT],
                         start=(c == 0), stop=(c == KC - 1), skip_group_check=True)
            ins = None
            for c in range(KC):
                ins = t.matmul(cm.pB[0:1, :], cm.onesb[:, 0:1], cm.sq[:, c * TT:(c + 1) * TT],
                               start=(c == 0), stop=(c == KC - 1), skip_group_check=True)
            return ins
        q.run("t", mms)
        q.run("v", lambda v, j=j: [
            v.tensor_copy(cm.rows[0:1, j * TT:(j + 1) * TT], cm.pA[0:1, :]),
            v.tensor_copy(cm.rows[0:1, TK + j * TT:TK + (j + 1) * TT], cm.pB[0:1, :])])

    def rowmath(v):
        v.tensor_scalar_mul(cm.rows[0:1, 0:ncols], cm.rows[0:1, 0:ncols], 1.0 / D)
        v.tensor_scalar_mul(cm.rows[0:1, TK:TK + ncols],
                            cm.rows[0:1, TK:TK + ncols], 1.0 / D)
        v.tensor_tensor(cm.xs[0:1, 0:ncols], cm.rows[0:1, 0:ncols],
                        cm.rows[0:1, 0:ncols], ALU.mult)
        v.tensor_sub(cm.rows[0:1, TK:TK + ncols], cm.rows[0:1, TK:TK + ncols],
                     cm.xs[0:1, 0:ncols])
        return v.tensor_scalar_add(cm.rows[0:1, TK:TK + ncols],
                                   cm.rows[0:1, TK:TK + ncols], LN_EPS)
    q.run("v", rowmath)
    q.run("s", lambda s: s.activation(cm.rows[0:1, TK:TK + ncols],
                                      cm.rows[0:1, TK:TK + ncols], AF.Sqrt))
    q.run("v", lambda v: v.reciprocal(cm.rows[0:1, TK:TK + ncols],
                                      cm.rows[0:1, TK:TK + ncols]))

    for j in range(nt):
        dma_step(q, "y", [(cm.xs[:, c * TT:(c + 1) * TT],
                           x_h[c * P:(c + 1) * P, j * TT:(j + 1) * TT]) for c in range(KC)])

        def bcasts(t, j=j):
            t.matmul(cm.pA[:], cm.onesr[0:1, 0:P],
                     cm.rows[0:1, j * TT:(j + 1) * TT],
                     start=True, stop=True, skip_group_check=True)
            return t.matmul(cm.pB[:], cm.onesr[0:1, 0:P],
                            cm.rows[0:1, TK + j * TT:TK + (j + 1) * TT],
                            start=True, stop=True, skip_group_check=True)
        q.run("t", bcasts)

        def norm(v):
            ins = None
            for c in range(KC):
                sl = slice(c * TT, (c + 1) * TT)
                v.tensor_sub(cm.xs[:, sl], cm.xs[:, sl], cm.pA[:])
                ins = v.tensor_mul(cm.xbs[:, sl], cm.xs[:, sl], cm.pB[:])
            return ins
        q.run("v", norm)
        dma_step(q, "g", [(xn_d[c * P:(c + 1) * P, j * TT:(j + 1) * TT],
                           cm.xbs[:, c * TT:(c + 1) * TT]) for c in range(KC)])


def load_w(q, cm, w_h_list, wbf, m):
    kc = len(w_h_list)
    per = max(1, 2048 // m)
    for c0 in range(0, kc, per):
        n = min(per, kc - c0)
        dma_step(q, "y", [(cm.wf[:, i * m:(i + 1) * m], w_h_list[c0 + i]) for i in range(n)])
        q.run("v", lambda v, c0=c0, n=n: [
            v.tensor_copy(wbf[:, (c0 + i) * m:(c0 + i + 1) * m], cm.wf[:, i * m:(i + 1) * m])
            for i in range(n)])


def proj_fm(q, cm, wbf, bias_off, x_d, out_d, m_total, ncols, xoff=0):
    nt = ncols // TT
    MC = m_total // P
    banks = [cm.pA, cm.pB, cm.pC, cm.pD]
    for j in range(nt):
        dma_step(q, "y", [(cm.xbs[:, c * TT:(c + 1) * TT],
                           x_d[c * P:(c + 1) * P, xoff + j * TT:xoff + (j + 1) * TT])
                          for c in range(KC)])
        for g0 in range(0, MC, 4):
            gsz = min(4, MC - g0)

            def mm(t, g0=g0, gsz=gsz):
                ins = None
                for i in range(gsz):
                    mc = g0 + i
                    pp = banks[i]
                    for c in range(KC):
                        t.matmul(pp[:], wbf[:, c * m_total + mc * P:c * m_total + (mc + 1) * P],
                                 cm.xbs[:, c * TT:(c + 1) * TT],
                                 start=(c == 0), stop=False, skip_group_check=True)
                    ins = t.matmul(pp[:], cm.bias[0:1, bias_off + mc * P:bias_off + (mc + 1) * P],
                                   cm.onesr[0:1, :], start=False, stop=True, skip_group_check=True)
                return ins
            q.run("t", mm)
            q.run("v", lambda v, g0=g0, gsz=gsz: [
                v.tensor_copy(cm.ot[:, (g0 + i) * TT:(g0 + i + 1) * TT], banks[i][:])
                for i in range(gsz)])
        dma_step(q, "g", [(out_d[mc * P:(mc + 1) * P, j * TT:(j + 1) * TT],
                           cm.ot[:, mc * TT:(mc + 1) * TT]) for mc in range(MC)])


def proj_tm(q, cm, wbf, bias_off, x_d, out_d, ncols_tok, fdim=D):
    FG = fdim // TT
    for jt in range(ncols_tok // TT):
        dma_step(q, "y", [(cm.xbs[:, c * TT:(c + 1) * TT],
                           x_d[c * P:(c + 1) * P, jt * TT:(jt + 1) * TT]) for c in range(KC)])
        for sub in range(TT // P):
            def mmtm(t, sub=sub):
                ins = None
                for half in range(FG):
                    pp = [cm.pA, cm.pB][half % 2]
                    for c in range(KC):
                        t.matmul(pp[:], cm.xbs[:, c * TT + sub * P:c * TT + (sub + 1) * P],
                                 wbf[:, c * fdim + half * TT:c * fdim + (half + 1) * TT],
                                 start=(c == 0), stop=False, skip_group_check=True)
                    ins = t.matmul(pp[:], cm.onesr[0:1, 0:P],
                                   cm.bias[0:1, bias_off + half * TT:bias_off + (half + 1) * TT],
                                   start=False, stop=True, skip_group_check=True)
                return ins
            q.run("t", mmtm)
            q.run("v", lambda v, sub=sub: [
                v.tensor_copy(cm.ot[:, half * TT:(half + 1) * TT], [cm.pA, cm.pB][half % 2][:])
                for half in range(FG)])
            dma_step(q, "g", [(out_d[jt * TT + sub * P:jt * TT + (sub + 1) * P, :],
                               cm.ot[:, 0:fdim])])


def build_L1(toff):
    nc = bass.Bass("TRN2", target_bir_lowering=False)
    dt = nc.dram_tensor
    xb_h = dt("xb", [D, TK], F32, kind="ExternalInput").ap()
    wq_h = dt("wq", [KC, P, D], F32, kind="ExternalInput").ap()
    wk_h = dt("wk", [KC, P, D], F32, kind="ExternalInput").ap()
    wv_h = dt("wv", [KC, P, D], F32, kind="ExternalInput").ap()
    bq_h = dt("bq", [1, D], F32, kind="ExternalInput").ap()
    bk_h = dt("bk", [1, D], F32, kind="ExternalInput").ap()
    bv_h = dt("bv", [1, D], F32, kind="ExternalInput").ap()
    woT_h = dt("woT", [KC, P, D], F32, kind="ExternalInput").ap()
    bo_h = dt("bo", [1, D], F32, kind="ExternalInput").ap()
    rW_h = dt("rW", [P, 2 * KC * 8], F32, kind="ExternalInput").ap()
    rb_h = dt("rb", [2, 1, 8], F32, kind="ExternalInput").ap()
    gb_h = dt("gb", [2, 1, 8], F32, kind="ExternalInput").ap()
    eW_h = dt("eW", [6, 3, KC, P, D], F32, kind="ExternalInput").ap()
    ebup_h = dt("ebup", [6, 1, D], F32, kind="ExternalInput").ap()
    ebg_h = dt("ebg", [6, 1, D], F32, kind="ExternalInput").ap()
    ebdnT_h = dt("ebdnT", [8, D], F32, kind="ExternalInput").ap()
    sWup_h = dt("sWup", [2, KC, P, D], F32, kind="ExternalInput").ap()
    sbup_h = dt("sbup", [2, 1, D], F32, kind="ExternalInput").ap()
    sWg_h = dt("sWg", [KC, P, D], F32, kind="ExternalInput").ap()
    sbg_h = dt("sbg", [1, D], F32, kind="ExternalInput").ap()
    sWdn_h = dt("sWdn", [KC, P, D], F32, kind="ExternalInput").ap()
    sbdn_h = dt("sbdn", [1, D], F32, kind="ExternalInput").ap()
    normw_h = dt("normw", [P, 16], F32, kind="ExternalInput").ap()
    id_h = dt("idm", [P, P], F32, kind="ExternalInput").ap()
    selc_h = dt("selc", [8, 6 * P], F32, kind="ExternalInput").ap()
    xout_h = dt("xout", [D, T], F32, kind="ExternalOutput").ap()

    xn_d = dt("xn_d", [D, TK], BF16).ap()
    q_d = dt("q_d", [D, T], BF16).ap()
    k_d = dt("k_d", [TK, D], BF16).ap()
    v_d = dt("v_d", [TK, D], BF16).ap()
    xa_d = dt("xa_d", [D, T], F32).ap()
    y_d = dt("y_d", [2, D, T], BF16).ap()

    st = ExitStack()
    q = Seq(nc, st)
    with st:
        cm = Common(q)
        wbf = q.sb("wbf", P, KC * D, BF16)
        wbf2 = q.sb("wbf2", P, KC * D, BF16)
        wbf3 = q.sb("wbf3", P, KC * D, BF16)
        ys = q.sb("ys", P, KC * TT, BF16)
        h1 = q.sb("h1", P, KC * TT, BF16)
        sg = cm.sq
        comb = q.sb("comb", P, KC * T, BF16)
        kv_s = q.sb("kv_s", P, D, BF16)
        kvp = q.ps("kvp", P, D)
        gates = q.sb("gates", P, 16 * 8, F32)
        scr8 = q.sb("scr8", P, 8 * 8, F32)
        negbig = q.sb("negbig", P, 8, F32)
        wtok = q.sb("wtok", P, 8, F32)
        wfm = q.sb("wfm", 8, T, BF16)
        gbt = q.sb("gbt", P, 16, F32)
        ebdnT_s = q.sb("ebdnT_s", 8, D, BF16)
        normw_s = q.sb("normw_s", P, 16, F32)
        rW_s = q.sb("rW_s", P, 2 * KC * 8, BF16)
        id_s = q.sb("id_s", P, P, F32)
        sel_s = q.sb("sel_s", 8, 6 * P, BF16)
        pT = q.ps("pT", P, P)

        q.run("v", lambda v: v.memset(negbig[:], -1e30), after=None)

        ln_to_dram(q, cm, xb_h, xn_d, TK)

        dma_step(q, "y", [(cm.bias[0:1, 0:D], bq_h), (cm.bias[0:1, 1024:1024 + D], bk_h),
                          (cm.bias[0:1, 2048:2048 + D], bv_h), (cm.bias[0:1, 3072:3072 + D], bo_h),
                          (id_s[:], id_h)])
        load_w(q, cm, [wq_h[c] for c in range(KC)], wbf, D)
        proj_fm(q, cm, wbf, 0, xn_d, q_d, D, T, xoff=toff)

        load_w(q, cm, [wk_h[c] for c in range(KC)], wbf, D)
        load_w(q, cm, [wv_h[c] for c in range(KC)], wbf2, D)
        for jt in range(TK // TT):
            dma_step(q, "y", [(cm.xbs[:, c * TT:(c + 1) * TT],
                               xn_d[c * P:(c + 1) * P, jt * TT:(jt + 1) * TT])
                              for c in range(KC)])
            for sub in range(TT // P):
                for (w_, boff, dst) in ((wbf, 1024, k_d), (wbf2, 2048, v_d)):
                    def mmtm(t, sub=sub, w_=w_, boff=boff):
                        ins = None
                        for half in range(2):
                            pp = [cm.pA, cm.pB][half]
                            for c in range(KC):
                                t.matmul(pp[:], cm.xbs[:, c * TT + sub * P:c * TT + (sub + 1) * P],
                                         w_[:, c * D + half * TT:c * D + (half + 1) * TT],
                                         start=(c == 0), stop=False, skip_group_check=True)
                            ins = t.matmul(pp[:], cm.onesr[0:1, 0:P],
                                           cm.bias[0:1, boff + half * TT:boff + (half + 1) * TT],
                                           start=False, stop=True, skip_group_check=True)
                        return ins
                    q.run("t", mmtm)
                    q.run("v", lambda v: [v.tensor_copy(cm.ot[:, 0:TT], cm.pA[:]),
                                          v.tensor_copy(cm.ot[:, TT:2 * TT], cm.pB[:])])
                    dma_step(q, "g", [(dst[jt * TT + sub * P:jt * TT + (sub + 1) * P, :],
                                       cm.ot[:, 0:D])])

        for jt in range(TK // P):
            dma_step(q, "y", [(h1[:, 0:D], k_d[jt * P:(jt + 1) * P, :]),
                              (sg[:, 0:D], v_d[jt * P:(jt + 1) * P, :])])

            def mmkv(t, jt=jt):
                ins = None
                for hh in range(8):
                    ins = t.matmul(kvp[:, hh * P:(hh + 1) * P], h1[:, hh * P:(hh + 1) * P],
                                   sg[:, hh * P:(hh + 1) * P],
                                   start=(jt == 0 and hh % 4 == 0),
                                   stop=(jt == TK // P - 1),
                                   skip_group_check=True)
                return ins
            q.run("t", mmkv)
        q.run("v", lambda v: v.tensor_copy(kv_s[:], kvp[:]))

        load_w(q, cm, [woT_h[c] for c in range(KC)], wbf3, D)
        for j in range(T // TT):
            dma_step(q, "y", [(cm.xbs[:, c * TT:(c + 1) * TT],
                               q_d[c * P:(c + 1) * P, j * TT:(j + 1) * TT]) for c in range(KC)])

            for g0 in (0, 4):
                def mmat4(t, g0=g0):
                    ins = None
                    for i in range(4):
                        hh = g0 + i
                        ins = t.matmul([cm.pA, cm.pB, cm.pC, cm.pD][i][:],
                                       kv_s[:, hh * P:(hh + 1) * P],
                                       cm.xbs[:, hh * TT:(hh + 1) * TT],
                                       start=True, stop=True, skip_group_check=True)
                    return ins
                q.run("t", mmat4)
                q.run("v", lambda v, g0=g0: [
                    v.tensor_copy(h1[:, (g0 + i) * TT:(g0 + i + 1) * TT],
                                  [cm.pA, cm.pB, cm.pC, cm.pD][i][:]) for i in range(4)])
            dma_step(q, "y", [(cm.xs[:, c * TT:(c + 1) * TT],
                               xb_h[c * P:(c + 1) * P, toff + j * TT:toff + (j + 1) * TT])
                              for c in range(KC)])
            for g0 in (0, 4):
                def mmo(t, g0=g0):
                    ins = None
                    for i in range(4):
                        dc = g0 + i
                        pp = [cm.pA, cm.pB, cm.pC, cm.pD][i]
                        for c in range(KC):
                            t.matmul(pp[:], wbf3[:, c * D + dc * P:c * D + (dc + 1) * P],
                                     h1[:, c * TT:(c + 1) * TT],
                                     start=(c == 0), stop=False, skip_group_check=True)
                        ins = t.matmul(pp[:], cm.bias[0:1, 3072 + dc * P:3072 + (dc + 1) * P],
                                       cm.onesr[0:1, :], start=False, stop=True,
                                       skip_group_check=True)
                    return ins
                q.run("t", mmo)
                q.run("v", lambda v, g0=g0: [
                    v.tensor_add(cm.xs[:, (g0 + i) * TT:(g0 + i + 1) * TT],
                                 cm.xs[:, (g0 + i) * TT:(g0 + i + 1) * TT],
                                 [cm.pA, cm.pB, cm.pC, cm.pD][i][:]) for i in range(4)])
            dma_step(q, "g", [(xa_d[c * P:(c + 1) * P, j * TT:(j + 1) * TT],
                               cm.xs[:, c * TT:(c + 1) * TT]) for c in range(KC)])

        dma_step(q, "y", [(cm.wf[0:8, 0:D], ebdnT_h[:]), (normw_s[:], normw_h[:]),
                          (cm.wf[:, D:D + 2 * KC * 8], rW_h[:]),
                          (cm.bias[0:1, 4096:4096 + 8], rb_h[0]),
                          (cm.bias[0:1, 4104:4104 + 8], rb_h[1]),
                          (cm.bias[0:1, 4112:4112 + 8], gb_h[0]),
                          (cm.bias[0:1, 4120:4120 + 8], gb_h[1]),
                          (cm.bias[0:1, 3072:3072 + D], sbdn_h)])
        dma_step(q, "y", [(cm.wf[0:8, D + 128:D + 128 + 6 * P], selc_h)])
        q.run("v", lambda v: [v.tensor_copy(ebdnT_s[:], cm.wf[0:8, 0:D]),
                              v.tensor_copy(rW_s[:], cm.wf[:, D:D + 2 * KC * 8]),
                              v.tensor_copy(sel_s[:], cm.wf[0:8, D + 128:D + 128 + 6 * P])])
        def gbmm(t):
            t.matmul(cm.pA[:, 0:8], cm.onesr[0:1, 0:P], cm.bias[0:1, 4112:4120],
                     start=True, stop=True, skip_group_check=True)
            return t.matmul(cm.pB[:, 0:8], cm.onesr[0:1, 0:P], cm.bias[0:1, 4120:4128],
                            start=True, stop=True, skip_group_check=True)
        q.run("t", gbmm)
        q.run("v", lambda v: [v.tensor_copy(gbt[:, 0:8], cm.pA[:, 0:8]),
                              v.tensor_copy(gbt[:, 8:16], cm.pB[:, 0:8])])

        for cy in range(2):
            ysrc = xn_d if cy == 0 else y_d[0]
            yoff = toff if cy == 0 else 0

            for jt in range(T // P):
                dma_step(q, "y", [(cm.xbs[:, c * P:(c + 1) * P],
                                   ysrc[c * P:(c + 1) * P, yoff + jt * P:yoff + (jt + 1) * P])
                                  for c in range(KC)])

                def mmg(t, jt=jt, cy=cy):
                    for c in range(KC):
                        t.matmul(cm.pA[:, 0:8], cm.xbs[:, c * P:(c + 1) * P],
                                 rW_s[:, (cy * KC + c) * 8:(cy * KC + c + 1) * 8],
                                 start=(c == 0), stop=False, skip_group_check=True)
                    return t.matmul(cm.pA[:, 0:8], cm.onesr[0:1, 0:P],
                                    cm.bias[0:1, 4096 + cy * 8:4104 + cy * 8],
                                    start=False, stop=True, skip_group_check=True)
                q.run("t", mmg)
                q.run("v", lambda v, jt=jt: v.tensor_copy(gates[:, jt * 8:(jt + 1) * 8],
                                                          cm.pA[:, 0:8]))

            for jt in range(T // P):
                gsl = gates[:, jt * 8:(jt + 1) * 8]

                def topk(v, jt=jt, cy=cy):
                    b = scr8[:, 0:8]
                    m1 = scr8[:, 8:9]
                    mask1 = scr8[:, 16:24]
                    b2 = scr8[:, 24:32]
                    m2 = scr8[:, 32:33]
                    sel = scr8[:, 40:48]
                    v.tensor_add(b, gsl, gbt[:, cy * 8:(cy + 1) * 8])
                    return None
                q.run("v", topk)
                q.run("v", lambda v: v.reduce_max(scr8[:, 8:9], scr8[:, 0:8], axis=AXX))
                q.run("v", lambda v: [
                    v.tensor_tensor(scr8[:, 16:24], scr8[:, 0:8],
                                    scr8[:, 8:9].broadcast_to([P, 8]), ALU.is_equal)])
                q.run("v", lambda v: [
                    v.tensor_scalar_mul(scr8[:, 24:32], scr8[:, 16:24], -1e30),
                    v.tensor_add(scr8[:, 24:32], scr8[:, 24:32], scr8[:, 0:8])])
                q.run("v", lambda v: v.reduce_max(scr8[:, 32:33], scr8[:, 24:32], axis=AXX))
                q.run("v", lambda v: [
                    v.tensor_tensor(scr8[:, 40:48], scr8[:, 24:32],
                                    scr8[:, 32:33].broadcast_to([P, 8]), ALU.is_equal),
                    v.tensor_add(scr8[:, 40:48], scr8[:, 40:48], scr8[:, 16:24])])
                q.run("s", lambda s, jt=jt: s.activation(scr8[:, 48:56], gates[:, jt * 8:(jt + 1) * 8],
                                                         AF.Exp))
                q.run("v", lambda v: v.tensor_mul(scr8[:, 48:56], scr8[:, 48:56], scr8[:, 40:48]))
                q.run("v", lambda v: v.reduce_sum(scr8[:, 56:57], scr8[:, 48:56], axis=AXX))
                q.run("v", lambda v: v.reciprocal(scr8[:, 57:58], scr8[:, 56:57]))
                q.run("v", lambda v: v.tensor_scalar(wtok[:], scr8[:, 48:56], scr8[:, 57:58],
                                                     None, ALU.mult))
                q.run("t", lambda t: t.transpose(pT[0:8, 0:P], wtok[:], id_s[:]))
                q.run("v", lambda v, jt=jt: v.tensor_copy(wfm[0:8, jt * P:(jt + 1) * P], pT[0:8, 0:P]))

            for tt in range(T // TT):
                for g0 in (0, 4):
                    def mmbd4(t, tt=tt, g0=g0):
                        ins = None
                        for i in range(4):
                            dc = g0 + i
                            pp = [cm.pA, cm.pB, cm.pC, cm.pD][i]
                            t.matmul(pp[:], ebdnT_s[0:6, dc * P:(dc + 1) * P],
                                     wfm[0:6, tt * TT:(tt + 1) * TT],
                                     start=True, stop=False, skip_group_check=True)
                            t.matmul(pp[:], cm.bias[0:1, 3072 + dc * P:3072 + (dc + 1) * P],
                                     cm.onesr[0:1, :], start=False, stop=False, skip_group_check=True)
                            ins = t.matmul(pp[:], cm.bias[0:1, 3072 + dc * P:3072 + (dc + 1) * P],
                                           cm.onesr[0:1, :], start=False, stop=True, skip_group_check=True)
                        return ins
                    q.run("t", mmbd4)
                    q.run("v", lambda v, tt=tt, g0=g0: [
                        v.tensor_copy(comb[:, (g0 + i) * T + tt * TT:(g0 + i) * T + (tt + 1) * TT],
                                      [cm.pA, cm.pB, cm.pC, cm.pD][i][:]) for i in range(4)])

            for e in range(6):
                load_w(q, cm, [eW_h[e, 0, c] for c in range(KC)], wbf, D)
                load_w(q, cm, [eW_h[e, 1, c] for c in range(KC)], wbf2, D)
                load_w(q, cm, [eW_h[e, 2, c] for c in range(KC)], wbf3, D)
                dma_step(q, "y", [(cm.bias[0:1, 0:D], ebup_h[e]),
                                  (cm.bias[0:1, 1024:1024 + D], ebg_h[e])])
                for tt in range(T // TT):
                    dma_step(q, "y", [(ys[:, c * TT:(c + 1) * TT],
                                       ysrc[c * P:(c + 1) * P, yoff + tt * TT:yoff + (tt + 1) * TT])
                                      for c in range(KC)])
                    for g0 in (0, 4):
                        def mmup(t, g0=g0):
                            ins = None
                            for i in range(4):
                                mc = g0 + i
                                pp = [cm.pA, cm.pB, cm.pC, cm.pD][i]
                                for c in range(KC):
                                    t.matmul(pp[:], wbf[:, c * D + mc * P:c * D + (mc + 1) * P],
                                             ys[:, c * TT:(c + 1) * TT],
                                             start=(c == 0), stop=False, skip_group_check=True)
                                ins = t.matmul(pp[:], cm.bias[0:1, mc * P:(mc + 1) * P],
                                               cm.onesr[0:1, :], start=False, stop=True,
                                               skip_group_check=True)
                            return ins
                        q.run("t", mmup)
                        q.run("v", lambda v, g0=g0: [
                            v.tensor_copy(h1[:, (g0 + i) * TT:(g0 + i + 1) * TT],
                                          [cm.pA, cm.pB, cm.pC, cm.pD][i][:]) for i in range(4)])
                    for g0 in (0, 4):
                        def mmgt(t, g0=g0):
                            ins = None
                            for i in range(4):
                                mc = g0 + i
                                pp = [cm.pA, cm.pB, cm.pC, cm.pD][i]
                                for c in range(KC):
                                    t.matmul(pp[:], wbf2[:, c * D + mc * P:c * D + (mc + 1) * P],
                                             h1[:, c * TT:(c + 1) * TT],
                                             start=(c == 0), stop=False, skip_group_check=True)
                                ins = t.matmul(pp[:], cm.bias[0:1, 1024 + mc * P:1024 + (mc + 1) * P],
                                               cm.onesr[0:1, :], start=False, stop=True,
                                               skip_group_check=True)
                            return ins
                        q.run("t", mmgt)
                        q.run("s", lambda s, g0=g0: [
                            s.activation(sg[:, (g0 + i) * TT:(g0 + i + 1) * TT],
                                         [cm.pA, cm.pB, cm.pC, cm.pD][i][:], AF.Silu)
                            for i in range(4)])
                    q.run("t", lambda t, e=e, tt=tt: t.matmul(
                        cm.pA[:], sel_s[0:8, e * P:(e + 1) * P],
                        wfm[0:8, tt * TT:(tt + 1) * TT],
                        start=True, stop=True, skip_group_check=True))
                    q.run("v", lambda v: [vv for c in range(KC)
                                          for vv in (v.tensor_mul(sg[:, c * TT:(c + 1) * TT],
                                                                  sg[:, c * TT:(c + 1) * TT],
                                                                  h1[:, c * TT:(c + 1) * TT]),)])
                    q.run("v", lambda v: [v.tensor_mul(h1[:, c * TT:(c + 1) * TT],
                                                       sg[:, c * TT:(c + 1) * TT], cm.pA[:])
                                          for c in range(KC)])
                    for g0 in (0, 4):
                        def mmdn(t, g0=g0):
                            ins = None
                            for i in range(4):
                                dc = g0 + i
                                pp = [cm.pA, cm.pB, cm.pC, cm.pD][i]
                                for c in range(KC):
                                    ins = t.matmul(pp[:], wbf3[:, c * D + dc * P:c * D + (dc + 1) * P],
                                                   h1[:, c * TT:(c + 1) * TT],
                                                   start=(c == 0), stop=(c == KC - 1),
                                                   skip_group_check=True)
                            return ins
                        q.run("t", mmdn)
                        q.run("v", lambda v, tt=tt, g0=g0: [
                            v.tensor_add(comb[:, (g0 + i) * T + tt * TT:(g0 + i) * T + (tt + 1) * TT],
                                         comb[:, (g0 + i) * T + tt * TT:(g0 + i) * T + (tt + 1) * TT],
                                         [cm.pA, cm.pB, cm.pC, cm.pD][i][:]) for i in range(4)])

            load_w(q, cm, [sWg_h[c] for c in range(KC)], wbf2, D)
            load_w(q, cm, [sWdn_h[c] for c in range(KC)], wbf3, D)
            dma_step(q, "y", [(cm.bias[0:1, 1024:1024 + D], sbg_h)])
            for ns in range(2):
                load_w(q, cm, [sWup_h[ns, c] for c in range(KC)], wbf, D)
                dma_step(q, "y", [(cm.bias[0:1, 0:D], sbup_h[ns])])
                for tt in range(T // TT):
                    dma_step(q, "y", [(ys[:, c * TT:(c + 1) * TT],
                                       ysrc[c * P:(c + 1) * P, yoff + tt * TT:yoff + (tt + 1) * TT])
                                      for c in range(KC)])
                    for g0 in (0, 4):
                        def mmup(t, g0=g0):
                            ins = None
                            for i in range(4):
                                mc = g0 + i
                                pp = [cm.pA, cm.pB, cm.pC, cm.pD][i]
                                for c in range(KC):
                                    t.matmul(pp[:], wbf[:, c * D + mc * P:c * D + (mc + 1) * P],
                                             ys[:, c * TT:(c + 1) * TT],
                                             start=(c == 0), stop=False, skip_group_check=True)
                                ins = t.matmul(pp[:], cm.bias[0:1, mc * P:(mc + 1) * P],
                                               cm.onesr[0:1, :], start=False, stop=True,
                                               skip_group_check=True)
                            return ins
                        q.run("t", mmup)
                        q.run("v", lambda v, g0=g0: [
                            v.tensor_copy(h1[:, (g0 + i) * TT:(g0 + i + 1) * TT],
                                          [cm.pA, cm.pB, cm.pC, cm.pD][i][:]) for i in range(4)])
                    for g0 in (0, 4):
                        def mmgt(t, g0=g0):
                            ins = None
                            for i in range(4):
                                mc = g0 + i
                                pp = [cm.pA, cm.pB, cm.pC, cm.pD][i]
                                for c in range(KC):
                                    t.matmul(pp[:], wbf2[:, c * D + mc * P:c * D + (mc + 1) * P],
                                             h1[:, c * TT:(c + 1) * TT],
                                             start=(c == 0), stop=False, skip_group_check=True)
                                ins = t.matmul(pp[:], cm.bias[0:1, 1024 + mc * P:1024 + (mc + 1) * P],
                                               cm.onesr[0:1, :], start=False, stop=True,
                                               skip_group_check=True)
                            return ins
                        q.run("t", mmgt)
                        q.run("s", lambda s, g0=g0: [
                            s.activation(sg[:, (g0 + i) * TT:(g0 + i + 1) * TT],
                                         [cm.pA, cm.pB, cm.pC, cm.pD][i][:], AF.Silu)
                            for i in range(4)])
                    q.run("v", lambda v: [v.tensor_mul(h1[:, c * TT:(c + 1) * TT],
                                                       sg[:, c * TT:(c + 1) * TT],
                                                       h1[:, c * TT:(c + 1) * TT])
                                          for c in range(KC)])
                    for g0 in (0, 4):
                        def mmdn(t, g0=g0):
                            ins = None
                            for i in range(4):
                                dc = g0 + i
                                pp = [cm.pA, cm.pB, cm.pC, cm.pD][i]
                                for c in range(KC):
                                    ins = t.matmul(pp[:], wbf3[:, c * D + dc * P:c * D + (dc + 1) * P],
                                                   h1[:, c * TT:(c + 1) * TT],
                                                   start=(c == 0), stop=(c == KC - 1),
                                                   skip_group_check=True)
                            return ins
                        q.run("t", mmdn)
                        q.run("v", lambda v, tt=tt, g0=g0: [
                            v.tensor_add(comb[:, (g0 + i) * T + tt * TT:(g0 + i) * T + (tt + 1) * TT],
                                         comb[:, (g0 + i) * T + tt * TT:(g0 + i) * T + (tt + 1) * TT],
                                         [cm.pA, cm.pB, cm.pC, cm.pD][i][:]) for i in range(4)])

            for tt in range(T // TT):
                dma_step(q, "y", [(ys[:, c * TT:(c + 1) * TT],
                                   ysrc[c * P:(c + 1) * P, yoff + tt * TT:yoff + (tt + 1) * TT])
                                  for c in range(KC)])
                q.run("v", lambda v, tt=tt: [
                    v.tensor_add(comb[:, c * T + tt * TT:c * T + (tt + 1) * TT],
                                 comb[:, c * T + tt * TT:c * T + (tt + 1) * TT],
                                 ys[:, c * TT:(c + 1) * TT]) for c in range(KC)])
                q.run("s", lambda s, tt=tt: [
                    s.activation(cm.sq[:, c * TT:(c + 1) * TT],
                                 comb[:, c * T + tt * TT:c * T + (tt + 1) * TT], AF.Square)
                    for c in range(KC)])

                def mmrms(t):
                    ins = None
                    for c in range(KC):
                        ins = t.matmul(cm.pA[0:1, :], cm.onesb[:, 0:1], cm.sq[:, c * TT:(c + 1) * TT],
                                       start=(c == 0), stop=(c == KC - 1), skip_group_check=True)
                    return ins
                q.run("t", mmrms)
                q.run("v", lambda v: v.tensor_copy(cm.rows[0:1, 0:TT], cm.pA[0:1, :]))
                q.run("v", lambda v: v.tensor_scalar(cm.rows[0:1, TT:2 * TT], cm.rows[0:1, 0:TT],
                                                     1.0 / D, 1e-6, ALU.mult, ALU.add))
                q.run("s", lambda s: s.activation(cm.rows[0:1, TT:2 * TT], cm.rows[0:1, TT:2 * TT],
                                                  AF.Sqrt))
                q.run("v", lambda v: v.reciprocal(cm.rows[0:1, TT:2 * TT], cm.rows[0:1, TT:2 * TT]))
                q.run("t", lambda t: t.matmul(cm.pB[:], cm.onesr[0:1, 0:P],
                                              cm.rows[0:1, TT:2 * TT], start=True, stop=True,
                                              skip_group_check=True))
                q.run("v", lambda v, tt=tt, cy=cy: [
                    vv for c in range(KC) for vv in (
                        v.tensor_mul(cm.xs[:, c * TT:(c + 1) * TT],
                                     comb[:, c * T + tt * TT:c * T + (tt + 1) * TT], cm.pB[:]),
                        v.tensor_scalar(cm.ot[:, c * TT:(c + 1) * TT], cm.xs[:, c * TT:(c + 1) * TT],
                                        normw_s[:, cy * 8 + c:cy * 8 + c + 1], None, ALU.mult))])
                dma_step(q, "g", [(y_d[cy, c * P:(c + 1) * P, tt * TT:(tt + 1) * TT],
                                   cm.ot[:, c * TT:(c + 1) * TT]) for c in range(KC)])

        for tt in range(T // TT):
            dma_step(q, "y", [(cm.xs[:, c * TT:(c + 1) * TT],
                               xa_d[c * P:(c + 1) * P, tt * TT:(tt + 1) * TT]) for c in range(KC)]
                     + [(ys[:, c * TT:(c + 1) * TT],
                         y_d[1, c * P:(c + 1) * P, tt * TT:(tt + 1) * TT]) for c in range(KC)])
            q.run("v", lambda v: [v.tensor_add(cm.xs[:, c * TT:(c + 1) * TT],
                                               cm.xs[:, c * TT:(c + 1) * TT],
                                               ys[:, c * TT:(c + 1) * TT]) for c in range(KC)])
            tstore = dma_step(q, "g", [(xout_h[c * P:(c + 1) * P, tt * TT:(tt + 1) * TT],
                                        cm.xs[:, c * TT:(c + 1) * TT]) for c in range(KC)])
        q.run("g", lambda g: None, after=tstore)
    return nc


def prep_L1(inputs):
    w_qkv = np.asarray(inputs["L1_attn_w_qkv"], np.float32)
    b_qkv = np.asarray(inputs["L1_attn_b_qkv"], np.float32)
    w_o = np.asarray(inputs["L1_attn_w_o"], np.float32)
    b_o = np.asarray(inputs["L1_attn_b_o"], np.float32)
    perm = np.arange(D).reshape(8, 128).T.reshape(-1)
    perm = np.array([(r % 128) * 8 + r // 128 for r in range(D)])
    out = {}
    for ci, nm in enumerate(("q", "k", "v")):
        out["w" + nm] = chunks8(w_qkv[ci * D + perm, :].T)
        out["b" + nm] = b_qkv[ci * D + perm].reshape(1, D)
    out["woT"] = chunks8(w_o.T)
    out["bo"] = b_o.reshape(1, D)
    rW = np.asarray(inputs["L1_ffn_router_W"], np.float32)
    rWp = np.zeros((2, D, 8), np.float32)
    rWp[:, :, 0:6] = rW
    out["rW"] = np.ascontiguousarray(rWp.reshape(2, 8, 128, 8).transpose(2, 0, 1, 3).reshape(P, 128))
    rb = np.asarray(inputs["L1_ffn_router_b"], np.float32)
    gb = np.asarray(inputs["L1_ffn_gate_bias"], np.float32)
    rbp = np.full((2, 1, 8), -1e30, np.float32); rbp[:, 0, 0:6] = rb
    gbp = np.zeros((2, 1, 8), np.float32); gbp[:, 0, 0:6] = gb
    out["rb"] = rbp
    out["gb"] = gbp
    eW = np.stack([np.stack([chunks8(np.asarray(inputs["L1_ffn_exp_Wup"])[e]),
                             chunks8(np.asarray(inputs["L1_ffn_exp_Wg"])[e]),
                             chunks8(np.asarray(inputs["L1_ffn_exp_Wdown"])[e])])
                   for e in range(6)])
    out["eW"] = np.ascontiguousarray(eW)
    out["ebup"] = np.asarray(inputs["L1_ffn_exp_bup"], np.float32).reshape(6, 1, D)
    out["ebg"] = np.asarray(inputs["L1_ffn_exp_bg"], np.float32).reshape(6, 1, D)
    ebdnT = np.zeros((8, D), np.float32)
    ebdnT[0:6] = np.asarray(inputs["L1_ffn_exp_bdown"], np.float32)
    out["ebdnT"] = ebdnT
    sWup = np.asarray(inputs["L1_ffn_share_Wup"], np.float32)
    out["sWup"] = np.stack([chunks8(sWup[:, 0:D]), chunks8(sWup[:, D:2 * D])])
    sbup = np.asarray(inputs["L1_ffn_share_bup"], np.float32)
    out["sbup"] = np.stack([sbup[0:D].reshape(1, D), sbup[D:2 * D].reshape(1, D)])
    out["sWg"] = chunks8(np.asarray(inputs["L1_ffn_share_Wg"], np.float32))
    out["sbg"] = np.asarray(inputs["L1_ffn_share_bg"], np.float32).reshape(1, D)
    out["sWdn"] = chunks8(np.asarray(inputs["L1_ffn_share_Wdown"], np.float32))
    out["sbdn"] = np.asarray(inputs["L1_ffn_share_bdown"], np.float32).reshape(1, D)
    out["normw"] = np.ascontiguousarray(
        np.asarray(inputs["L1_ffn_norm_w"], np.float32).reshape(2, 8, 128).transpose(2, 0, 1).reshape(P, 16))
    out["idm"] = np.eye(P, dtype=np.float32)
    selc = np.zeros((8, 6 * P), np.float32)
    for e in range(6):
        selc[e, e * P:(e + 1) * P] = 1.0
    out["selc"] = selc
    return out




def build_L3():
    nc = bass.Bass("TRN2", target_bir_lowering=False)
    dt = nc.dram_tensor
    xb_h = dt("xb", [D, TK], F32, kind="ExternalInput").ap()
    wq_h = dt("wq", [KC, P, D], F32, kind="ExternalInput").ap()
    wk_h = dt("wk", [KC, P, D], F32, kind="ExternalInput").ap()
    wv_h = dt("wv", [KC, P, D], F32, kind="ExternalInput").ap()
    woT_h = dt("woT", [KC, P, D], F32, kind="ExternalInput").ap()
    w1T_h = dt("w1T", [KC, P, 4 * D], F32, kind="ExternalInput").ap()
    b1_h = dt("b1", [1, 4 * D], F32, kind="ExternalInput").ap()
    w2T_h = dt("w2T", [4 * KC, P, D], F32, kind="ExternalInput").ap()
    b2_h = dt("b2", [1, D], F32, kind="ExternalInput").ap()
    scale_h = dt("scale", [1, 1], F32, kind="ExternalInput").ap()
    lorx_h = dt("lorx", [KC, P, 16], F32, kind="ExternalInput").ap()
    id_h = dt("idm", [P, P], F32, kind="ExternalInput").ap()
    xout_h = dt("xout", [D, T], F32, kind="ExternalOutput").ap()

    xn_d = dt("xn_d", [D, TK], BF16).ap()
    q3_d = dt("q3_d", [D, T], BF16).ap()
    k3_d = dt("k3_d", [D, TK], BF16).ap()
    v3_d = dt("v3_d", [TK, D], BF16).ap()
    at_d = dt("at_d", [D, T], BF16).ap()
    ao_d = dt("ao_d", [D, T], BF16).ap()
    h_d = dt("h_d", [4 * D, T], BF16).ap()

    st = ExitStack()
    q = Seq(nc, st)
    with st:
        cm = Common(q)
        wbf = q.sb("wbf", P, KC * D, BF16)
        et = q.sb("et", P, 32 * TT, BF16)
        kh = q.sb("kh", 64, TK, BF16)
        vh = q.sb("vh", P, 32 * 64, BF16)
        qh = q.sb("qh", 64, TT, BF16)
        lor = q.sb("lor", P, KC * 16, BF16)
        evT = q.sb("evT", 1, 8 * 8 * P, BF16)
        t1s = q.sb("t1s", 1, 4 * TT, BF16)
        us = q.sb("us", 1, 16 * TT, BF16)
        scs = q.sb("scs", 1, 2, F32)
        id_s = q.sb("id_s", P, P, F32)
        idb = q.sb("idb", P, P, BF16)
        wsm = q.sb("wsm", P, 32 * P, BF16)
        pE = q.ps("pE", P, TT, BF16)
        pF = q.ps("pF", P, TT)

        ln_to_dram(q, cm, xb_h, xn_d, TK)

        q.run("v", lambda v: v.memset(cm.bias[0:1, 0:4352], 0.0))
        load_w(q, cm, [wq_h[c] for c in range(KC)], wbf, D)
        proj_fm(q, cm, wbf, 2048, xn_d, q3_d, D, T, xoff=0)
        load_w(q, cm, [wk_h[c] for c in range(KC)], wbf, D)
        proj_fm(q, cm, wbf, 2048, xn_d, k3_d, D, TK, xoff=0)
        load_w(q, cm, [wv_h[c] for c in range(KC)], wbf, D)
        proj_tm(q, cm, wbf, 2048, xn_d, v3_d, TK, fdim=D)

        dma_step(q, "y", [(id_s[:], id_h), (scs[0:1, 0:1], scale_h)])
        q.run("v", lambda v: v.tensor_copy(idb[:], id_s[:]))
        SC = 0.125
        for hh in range(16):
            dma_step(q, "y", [(kh[:], k3_d[hh * 64:(hh + 1) * 64, :])]
                     + [(vh[:, kc * 64:(kc + 1) * 64],
                         v3_d[kc * P:(kc + 1) * P, hh * 64:(hh + 1) * 64]) for kc in range(32)])
            for qt in range(T // TT):
                dma_step(q, "y", [(qh[:], q3_d[hh * 64:(hh + 1) * 64, qt * TT:(qt + 1) * TT])])
                for kc0 in range(0, 32, 2):
                    def mmsc(t, kc0=kc0):
                        t.matmul(cm.pA[:], kh[:, kc0 * P:(kc0 + 1) * P], qh[:],
                                 start=True, stop=True, skip_group_check=True)
                        return t.matmul(cm.pB[:], kh[:, (kc0 + 1) * P:(kc0 + 2) * P], qh[:],
                                        start=True, stop=True, skip_group_check=True)
                    q.run("t", mmsc)
                    q.run("s", lambda s, kc0=kc0: [
                        s.activation(et[:, kc0 * TT:(kc0 + 1) * TT], cm.pA[:], AF.Exp, scale=SC),
                        s.activation(et[:, (kc0 + 1) * TT:(kc0 + 2) * TT], cm.pB[:], AF.Exp, scale=SC)])
                def mmsum(t):
                    ins = None
                    for kc in range(32):
                        ins = t.matmul(cm.pC[0:1, :], cm.onesb[:, 0:1], et[:, kc * TT:(kc + 1) * TT],
                                       start=(kc == 0), stop=(kc == 31), skip_group_check=True)
                    return ins
                q.run("t", mmsum)

                def mmfu(t):
                    ins = None
                    for kc in range(32):
                        ins = t.matmul(pF[0:64, :], vh[:, kc * 64:(kc + 1) * 64],
                                       et[:, kc * TT:(kc + 1) * TT],
                                       start=(kc == 0), stop=(kc == 31), skip_group_check=True)
                    return ins
                q.run("t", mmfu)
                q.run("v", lambda v: v.tensor_copy(cm.rows[0:1, 0:TT], cm.pC[0:1, :]))
                q.run("v", lambda v: v.reciprocal(cm.rows[0:1, 0:TT], cm.rows[0:1, 0:TT]))
                q.run("t", lambda t: t.matmul(cm.pD[0:64, :], cm.onesr[0:1, 0:64],
                                              cm.rows[0:1, 0:TT], start=True, stop=True,
                                              skip_group_check=True))
                q.run("v", lambda v: v.tensor_copy(cm.ot[0:64, 0:TT], pF[0:64, :]))
                q.run("v", lambda v: v.tensor_mul(cm.ot[0:64, 0:TT], cm.ot[0:64, 0:TT],
                                                  cm.pD[0:64, :]))
                dma_step(q, "g", [(at_d[hh * 64:(hh + 1) * 64, qt * TT:(qt + 1) * TT],
                                   cm.ot[0:64, 0:TT])])

        load_w(q, cm, [woT_h[c] for c in range(KC)], wbf, D)
        proj_fm(q, cm, wbf, 2048, at_d, ao_d, D, T, xoff=0)

        dma_step(q, "y", [(cm.xs[:, c * 16:(c + 1) * 16], lorx_h[c]) for c in range(KC)])
        q.run("s", lambda s: s.activation(cm.sq[:, 0:KC * 16], cm.xs[:, 0:KC * 16], AF.Square))

        def lnl(t):
            for c in range(KC):
                t.matmul(cm.pA[0:1, 0:16], cm.ones[:, 0:1], cm.xs[:, c * 16:(c + 1) * 16],
                         start=(c == 0), stop=(c == KC - 1), skip_group_check=True)
            ins = None
            for c in range(KC):
                ins = t.matmul(cm.pB[0:1, 0:16], cm.onesb[:, 0:1], cm.sq[:, c * 16:(c + 1) * 16],
                               start=(c == 0), stop=(c == KC - 1), skip_group_check=True)
            return ins
        q.run("t", lnl)

        def lrow(v):
            v.tensor_scalar_mul(cm.rows[0:1, 0:16], cm.pA[0:1, 0:16], 1.0 / D)
            v.tensor_scalar_mul(cm.rows[0:1, 16:32], cm.pB[0:1, 0:16], 1.0 / D)
            v.tensor_tensor(cm.rows[0:1, 32:48], cm.rows[0:1, 0:16], cm.rows[0:1, 0:16], ALU.mult)
            v.tensor_sub(cm.rows[0:1, 16:32], cm.rows[0:1, 16:32], cm.rows[0:1, 32:48])
            return v.tensor_scalar_add(cm.rows[0:1, 16:32], cm.rows[0:1, 16:32], LN_EPS)
        q.run("v", lrow)
        q.run("s", lambda s: s.activation(cm.rows[0:1, 16:32], cm.rows[0:1, 16:32], AF.Sqrt))
        q.run("v", lambda v: v.reciprocal(cm.rows[0:1, 16:32], cm.rows[0:1, 16:32]))

        def lbc(t):
            t.matmul(cm.pA[:, 0:16], cm.onesr[0:1, 0:P], cm.rows[0:1, 0:16],
                     start=True, stop=True, skip_group_check=True)
            return t.matmul(cm.pB[:, 0:16], cm.onesr[0:1, 0:P], cm.rows[0:1, 16:32],
                            start=True, stop=True, skip_group_check=True)
        q.run("t", lbc)

        def lnorm(v):
            ins = None
            for c in range(KC):
                v.tensor_sub(cm.xs[:, c * 16:(c + 1) * 16], cm.xs[:, c * 16:(c + 1) * 16],
                             cm.pA[:, 0:16])
                ins = v.tensor_mul(lor[:, c * 16:(c + 1) * 16], cm.xs[:, c * 16:(c + 1) * 16],
                                   cm.pB[:, 0:16])
            return ins
        q.run("v", lnorm)

        for bi, je in enumerate((0, 2, 4, 6, 8, 10, 12, 14)):
            for c0 in range(0, KC, 4):
                def trev(t, je=je, c0=c0):
                    ins = None
                    for i in range(4):
                        c = c0 + i
                        ins = t.transpose(pE[0:1, i * P:(i + 1) * P],
                                          lor[:, c * 16 + je:c * 16 + je + 1], idb[:])
                    return ins
                q.run("t", trev)
                q.run("v", lambda v, bi=bi, c0=c0: v.tensor_copy(
                    evT[0:1, (bi * KC + c0) * P:(bi * KC + c0 + 4) * P], pE[0:1, 0:4 * P]))

        dma_step(q, "y", [(cm.bias[0:1, 0:4 * D], b1_h)])
        for jt in range(T // TT):
            dma_step(q, "y", [(cm.xbs[:, c * TT:(c + 1) * TT],
                               xn_d[c * P:(c + 1) * P, jt * TT:(jt + 1) * TT]) for c in range(KC)])
            for pp_ in range(4):
                def mmt1(t, pp_=pp_):
                    ins = None
                    for c in range(KC):
                        ins = t.matmul(cm.pA[0:1, :],
                                       lor[:, c * 16 + 2 * pp_ + 1:c * 16 + 2 * pp_ + 2],
                                       cm.xbs[:, c * TT:(c + 1) * TT],
                                       start=(c == 0), stop=(c == KC - 1), skip_group_check=True)
                    return ins
                q.run("t", mmt1)
                q.run("v", lambda v, pp_=pp_: v.tensor_scalar(
                    t1s[0:1, pp_ * TT:(pp_ + 1) * TT],
                    cm.pA[0:1, :], scs[0:1, 0:1], None, ALU.mult))
            for mc0 in range(0, 32, 4):
                for ihalf in range(2):
                    pairs = [(cm.wf[:, (i - 2 * ihalf) * P * KC + c * P:
                                    (i - 2 * ihalf) * P * KC + (c + 1) * P],
                              w1T_h[c][:, (mc0 + i) * P:(mc0 + i + 1) * P])
                             for i in range(2 * ihalf, 2 * ihalf + 2) for c in range(KC)]
                    dma_step(q, "y", pairs)
                    q.run("v", lambda v, ihalf=ihalf: [
                        v.tensor_copy(wsm[:, (2 * ihalf + i) * P * KC + c * P:
                                          (2 * ihalf + i) * P * KC + (c + 1) * P],
                                      cm.wf[:, i * P * KC + c * P:i * P * KC + (c + 1) * P])
                        for i in range(2) for c in range(KC)])

                def mmw1(t, mc0=mc0, jt=jt):
                    ins = None
                    for i in range(4):
                        mc = mc0 + i
                        pp = [cm.pA, cm.pB, cm.pC, cm.pD][i]
                        for c in range(KC):
                            t.matmul(pp[:], wsm[:, i * P * KC + c * P:i * P * KC + (c + 1) * P],
                                     cm.xbs[:, c * TT:(c + 1) * TT],
                                     start=(c == 0), stop=False, skip_group_check=True)
                        t.matmul(pp[:], cm.bias[0:1, mc * P:(mc + 1) * P], cm.onesr[0:1, :],
                                 start=False, stop=False, skip_group_check=True)
                        lp = mc // 8
                        lc = mc % 8
                        ins = t.matmul(pp[:], evT[0:1, (lp * KC + lc) * P:(lp * KC + lc + 1) * P],
                                       t1s[0:1, lp * TT:(lp + 1) * TT],
                                       start=False, stop=True, skip_group_check=True)
                    return ins
                q.run("t", mmw1)
                q.run("s", lambda s, mc0=mc0: [
                    s.activation(et[:, (mc0 + i) * TT:(mc0 + i + 1) * TT],
                                 [cm.pA, cm.pB, cm.pC, cm.pD][i][:], AF.Gelu) for i in range(4)])
            dma_step(q, "g", [(h_d[mc * P:(mc + 1) * P, jt * TT:(jt + 1) * TT],
                               et[:, mc * TT:(mc + 1) * TT]) for mc in range(32)])

        dma_step(q, "y", [(cm.bias[0:1, 0:D], b2_h)])
        for jt in range(T // TT):
            dma_step(q, "y", [(et[:, ck * TT:(ck + 1) * TT],
                               h_d[ck * P:(ck + 1) * P, jt * TT:(jt + 1) * TT]) for ck in range(32)])
            for pp_ in range(4):
                for hp in range(4):
                    def mmu(t, pp_=pp_, hp=hp):
                        ins = None
                        for jc in range(KC):
                            ins = t.matmul(cm.pA[0:1, :],
                                           lor[:, jc * 16 + 9 + 2 * pp_:jc * 16 + 10 + 2 * pp_],
                                           et[:, (hp * KC + jc) * TT:(hp * KC + jc + 1) * TT],
                                           start=(jc == 0), stop=(jc == KC - 1),
                                           skip_group_check=True)
                        return ins
                    q.run("t", mmu)
                    q.run("v", lambda v, pp_=pp_, hp=hp: v.tensor_scalar(
                        us[0:1, (pp_ * 4 + hp) * TT:(pp_ * 4 + hp + 1) * TT],
                        cm.pA[0:1, :], scs[0:1, 0:1], None, ALU.mult))
            dma_step(q, "y", [(cm.xs[:, c * TT:(c + 1) * TT],
                               xb_h[c * P:(c + 1) * P, jt * TT:(jt + 1) * TT]) for c in range(KC)]
                     + [(cm.xbs[:, c * TT:(c + 1) * TT],
                         ao_d[c * P:(c + 1) * P, jt * TT:(jt + 1) * TT]) for c in range(KC)])
            q.run("v", lambda v: [v.tensor_add(cm.xs[:, c * TT:(c + 1) * TT],
                                               cm.xs[:, c * TT:(c + 1) * TT],
                                               cm.xbs[:, c * TT:(c + 1) * TT]) for c in range(KC)])
            for dc in range(KC):
                for ckh in range(2):
                    dma_step(q, "y", [(cm.wf[:, ck * P:(ck + 1) * P],
                                       w2T_h[16 * ckh + ck][:, dc * P:(dc + 1) * P])
                                      for ck in range(16)])
                    q.run("v", lambda v, ckh=ckh: [
                        v.tensor_copy(wsm[:, (16 * ckh + ck) * P:(16 * ckh + ck + 1) * P],
                                      cm.wf[:, ck * P:(ck + 1) * P]) for ck in range(16)])

                def mmw2(t, dc=dc, jt=jt):
                    for ck in range(32):
                        t.matmul(pF[:], wsm[:, ck * P:(ck + 1) * P], et[:, ck * TT:(ck + 1) * TT],
                                 start=(ck == 0), stop=False, skip_group_check=True)
                    t.matmul(pF[:], cm.bias[0:1, dc * P:(dc + 1) * P], cm.onesr[0:1, :],
                             start=False, stop=False, skip_group_check=True)
                    ins = None
                    lp = dc // 2
                    r0 = (dc % 2) * P
                    for hp in range(4):
                        bi = 4 + lp
                        base = bi * KC * P
                        ins = t.matmul(pF[:],
                                       evT[0:1, base + 4 * r0 + hp:base + 4 * r0 + hp + 4 * (P - 1) + 1:4],
                                       us[0:1, (lp * 4 + hp) * TT:(lp * 4 + hp + 1) * TT],
                                       start=False, stop=(hp == 3), skip_group_check=True)
                    return ins
                q.run("t", mmw2)
                q.run("v", lambda v, dc=dc: v.tensor_add(cm.xs[:, dc * TT:(dc + 1) * TT],
                                                         cm.xs[:, dc * TT:(dc + 1) * TT], pF[:]))
            tstore = dma_step(q, "g", [(xout_h[c * P:(c + 1) * P, jt * TT:(jt + 1) * TT],
                                        cm.xs[:, c * TT:(c + 1) * TT]) for c in range(KC)])
        q.run("g", lambda g: None, after=tstore)
    return nc


def prep_L3(inputs):
    w_qkv = np.asarray(inputs["L3_attn_w_qkv"], np.float32)
    w_out = np.asarray(inputs["L3_attn_w_out"], np.float32)
    out = {}
    out["wq"] = chunks8(w_qkv[0:D].T)
    out["wk"] = chunks8(w_qkv[D:2 * D].T)
    out["wv"] = chunks8(w_qkv[2 * D:3 * D].T)
    out["woT"] = chunks8(w_out.T)
    out["w1T"] = chunks8(np.asarray(inputs["L3_ffn_W1"], np.float32).T)
    out["b1"] = np.asarray(inputs["L3_ffn_b1"], np.float32).reshape(1, 4 * D)
    out["w2T"] = np.ascontiguousarray(
        np.asarray(inputs["L3_ffn_W2"], np.float32).T.reshape(4 * KC, P, D))
    out["b2"] = np.asarray(inputs["L3_ffn_b2"], np.float32).reshape(1, D)
    out["scale"] = np.asarray(inputs["L3_ffn_scale"], np.float32).reshape(1, 1)
    out["idm"] = np.eye(P, dtype=np.float32)
    return out


def run_L3(x_fm, inputs):
    w = prep_L3(inputs)
    nc = build_L3()
    in_maps = []
    for c in range(NCORES):
        b = c // 2
        lorx = np.ascontiguousarray(x_fm[b][:, 1:17].reshape(KC, P, 16))
        in_maps.append(dict(w, xb=_roll_x(x_fm, c), lorx=lorx))
    res = run_bass_kernel_spmd(nc, in_maps, core_ids=list(range(NCORES))).results
    x_new = np.empty_like(x_fm)
    _scatter_out(res, x_new)
    return x_new


def build_L0():
    nc = bass.Bass("TRN2", target_bir_lowering=False)
    dt = nc.dram_tensor
    xb_h = dt("xb", [D, T], F32, kind="ExternalInput").ap()
    wqr_h = dt("wqr", [2, KC, P, D], F32, kind="ExternalInput").ap()
    bqr_h = dt("bqr", [2, 1, D], F32, kind="ExternalInput").ap()
    wk_h = dt("wk", [KC, P, D], F32, kind="ExternalInput").ap()
    bk_h = dt("bk", [1, D], F32, kind="ExternalInput").ap()
    wv_h = dt("wv", [KC, P, D], F32, kind="ExternalInput").ap()
    bv_h = dt("bv", [1, D], F32, kind="ExternalInput").ap()
    selred_h = dt("selred", [P, KC * 16], F32, kind="ExternalInput").ap()
    selmix_h = dt("selmix", [16, KC * P], F32, kind="ExternalInput").ap()
    selout_h = dt("selout", [P, 2 * P], F32, kind="ExternalInput").ap()
    selrep_h = dt("selrep", [P, 32 * P], F32, kind="ExternalInput").ap()
    wc_h = dt("wc", [KC, P, P], F32, kind="ExternalInput").ap()
    bc_h = dt("bc", [1, P], F32, kind="ExternalInput").ap()
    tie_h = dt("tie", [1, T], F32, kind="ExternalInput").ap()
    w1a_h = dt("w1a", [KC, P, 4 * D], F32, kind="ExternalInput").ap()
    b1f_h = dt("b1f", [P, 32], F32, kind="ExternalInput").ap()
    w2a_h = dt("w2a", [4 * KC, P, D], F32, kind="ExternalInput").ap()
    b2a_h = dt("b2a", [P, D], F32, kind="ExternalInput").ap()
    xout_h = dt("xout", [D, T], F32, kind="ExternalOutput").ap()

    xn_d = dt("xn_d", [D, T], BF16).ap()
    qr_full = dt("qr_d", [2 * D, T], BF16)
    qr_d = qr_full.ap()
    qr_d2 = qr_full.ap()[D:2 * D, :]
    k0_d = dt("k0_d", [D, T], BF16).ap()
    v0_d = dt("v0_d", [D, T], BF16).ap()
    xa_d = dt("xa_d", [D, T], F32).ap()
    pm_d = dt("pm_d", [P, T], BF16).ap()
    z_d = dt("z_d", [4 * D, T // 32], BF16).ap()

    st = ExitStack()
    q = Seq(nc, st)
    with st:
        cm = Common(q)
        wbf = q.sb("wbf", P, KC * D, BF16)
        zp = q.sb("zp", P, 32 * TT, BF16)
        qr = q.sb("qr", P, 16 * TT, BF16)
        kt = q.sb("kt", P, KC * TT, BF16)
        vt = q.sb("vt", P, KC * TT, BF16)
        pr = q.sb("pr", P, TT, BF16)
        pr2 = q.sb("pr2", P, TT, BF16)
        ea = q.sb("ea", 16, TT, BF16)
        selred_s = q.sb("selred_s", P, KC * 16, BF16)
        selmix_s = q.sb("selmix_s", 16, KC * P, BF16)
        selout_s = q.sb("selout_s", P, 2 * P, BF16)
        selrep_s = q.sb("selrep_s", P, 32 * P, BF16)
        b1f_s = q.sb("b1f_s", P, 32, F32)
        b2a_s = q.sb("b2a_s", P, D, BF16)
        zz = q.sb("zz", P, 32 * 64, BF16)
        pmt = q.sb("pmt", P, TT, BF16)
        lg = q.sb("lg", P, TT, F32)
        mx = q.sb("mx", P, 16, F32)
        pS = q.ps("pS", 16, TT)
        pO = q.ps("pO", P, TT)

        ln_to_dram(q, cm, xb_h, xn_d, T)

        dma_step(q, "y", [(cm.bias[0:1, 0:D], bk_h), (cm.bias[0:1, 1024:2048], bv_h),
                          (cm.bias[0:1, 2048:3072], bqr_h[0]),
                          (cm.bias[0:1, 3072:4096], bqr_h[1])])
        load_w(q, cm, [wqr_h[0, c] for c in range(KC)], wbf, D)
        proj_fm(q, cm, wbf, 2048, xn_d, qr_d, D, T, xoff=0)
        load_w(q, cm, [wqr_h[1, c] for c in range(KC)], wbf, D)
        proj_fm(q, cm, wbf, 3072, xn_d, qr_d2, D, T, xoff=0)
        load_w(q, cm, [wk_h[c] for c in range(KC)], wbf, D)
        proj_fm(q, cm, wbf, 0, xn_d, k0_d, D, T, xoff=0)
        load_w(q, cm, [wv_h[c] for c in range(KC)], wbf, D)
        proj_fm(q, cm, wbf, 1024, xn_d, v0_d, D, T, xoff=0)

        dma_step(q, "y", [(cm.wf[:, 0:KC * 16], selred_h),
                          (cm.wf[0:16, KC * 16:KC * 16 + KC * P], selmix_h),
                          (cm.wf[:, 1536:1536 + 2 * P], selout_h)])
        q.run("v", lambda v: [v.tensor_copy(selred_s[:], cm.wf[:, 0:KC * 16]),
                              v.tensor_copy(selmix_s[:], cm.wf[0:16, KC * 16:KC * 16 + KC * P]),
                              v.tensor_copy(selout_s[:], cm.wf[:, 1536:1536 + 2 * P])])

        for tt in range(T // TT):
            dma_step(q, "y", [(kt[:, c * TT:(c + 1) * TT],
                               k0_d[c * P:(c + 1) * P, tt * TT:(tt + 1) * TT]) for c in range(KC)]
                     + [(vt[:, c * TT:(c + 1) * TT],
                         v0_d[c * P:(c + 1) * P, tt * TT:(tt + 1) * TT]) for c in range(KC)]
                     + [(qr[:, i * TT:(i + 1) * TT],
                         qr_d[i * P:(i + 1) * P, tt * TT:(tt + 1) * TT]) for i in range(16)]
                     + [(cm.xs[:, c * TT:(c + 1) * TT],
                         xb_h[c * P:(c + 1) * P, tt * TT:(tt + 1) * TT]) for c in range(KC)])
            for i in range(16):
                for c0 in (0, 4):
                    q.run("v", lambda v, i=i, c0=c0: [
                        v.tensor_mul(pr[:], qr[:, i * TT:(i + 1) * TT],
                                     kt[:, (c0 + ic) * TT:(c0 + ic + 1) * TT]) if False else None
                        for ic in range(1)])
                def sc_chain(i=i):
                    for c in range(KC):
                        q.run("v", lambda v, i=i, c=c: v.tensor_mul(
                            pr[:], qr[:, i * TT:(i + 1) * TT], kt[:, c * TT:(c + 1) * TT]))
                        q.run("t", lambda t, c=c: t.matmul(
                            pS[:], selred_s[:, c * 16:(c + 1) * 16], pr[:],
                            start=(c == 0), stop=(c == KC - 1), skip_group_check=True))
                sc_chain()
                q.run("s", lambda s: s.activation(ea[:], pS[:], AF.Exp, scale=0.125))
                q.run("t", lambda t: t.matmul(cm.pC[0:1, :], cm.onesb[0:16, 0:1], ea[:],
                                              start=True, stop=True, skip_group_check=True))
                q.run("v", lambda v: v.tensor_copy(cm.rows[0:1, 0:TT], cm.pC[0:1, :]))
                q.run("v", lambda v: v.reciprocal(cm.rows[0:1, 0:TT], cm.rows[0:1, 0:TT]))
                q.run("t", lambda t: t.matmul(cm.pD[0:16, :], cm.onesr[0:1, 0:16],
                                              cm.rows[0:1, 0:TT], start=True, stop=True,
                                              skip_group_check=True))
                q.run("v", lambda v: v.tensor_mul(ea[:], ea[:], cm.pD[0:16, :]))
                def mix_chain(i=i):
                    for c in range(KC):
                        q.run("t", lambda t, c=c: t.matmul(
                            cm.pA[:], selmix_s[:, c * P:(c + 1) * P], ea[:],
                            start=True, stop=True, skip_group_check=True))
                        q.run("v", lambda v, c=c: v.tensor_mul(
                            pr2[:], vt[:, c * TT:(c + 1) * TT], cm.pA[:]))
                        q.run("t", lambda t, c=c, i=i: t.matmul(
                            pO[:], selout_s[:, (i % 2) * P:(i % 2 + 1) * P], pr2[:],
                            start=(c == 0 and i % 2 == 0), stop=(c == KC - 1 and i % 2 == 1),
                            skip_group_check=True))
                mix_chain()
                if i % 2 == 1:
                    q.run("v", lambda v, i=i, tt=tt: v.tensor_add(
                        cm.xs[:, (i // 2) * TT:(i // 2 + 1) * TT],
                        cm.xs[:, (i // 2) * TT:(i // 2 + 1) * TT], pO[:]))
            dma_step(q, "g", [(xa_d[c * P:(c + 1) * P, tt * TT:(tt + 1) * TT],
                               cm.xs[:, c * TT:(c + 1) * TT]) for c in range(KC)])

        load_w(q, cm, [wc_h[c] for c in range(KC)], wbf, P)
        dma_step(q, "y", [(cm.bias[0:1, 0:P], bc_h)])
        dma_step(q, "y", [(cm.rows[0:1, TK:TK + T], tie_h)])
        for tt in range(T // TT):
            dma_step(q, "y", [(cm.xbs[:, c * TT:(c + 1) * TT],
                               xn_d[c * P:(c + 1) * P, tt * TT:(tt + 1) * TT]) for c in range(KC)])

            def mmlg(t, tt=tt):
                for c in range(KC):
                    t.matmul(cm.pA[:], wbf[:, c * P:(c + 1) * P], cm.xbs[:, c * TT:(c + 1) * TT],
                             start=(c == 0), stop=False, skip_group_check=True)
                t.matmul(cm.pA[:], cm.bias[0:1, 0:P], cm.onesr[0:1, :],
                         start=False, stop=False, skip_group_check=True)
                return t.matmul(cm.pA[:], cm.onesr[0:1, 0:P],
                                cm.rows[0:1, TK + tt * TT:TK + (tt + 1) * TT],
                                start=False, stop=True, skip_group_check=True)
            q.run("t", mmlg)
            q.run("v", lambda v: v.tensor_copy(lg[:], cm.pA[:]))
            q.run("v", lambda v: v.reduce_max(mx[:], lg[:].rearrange("p (g t) -> p g t", t=32),
                                              axis=AXX))
            q.run("v", lambda v: v.tensor_tensor(
                pmt[:], lg[:],
                mx[:].rearrange("p (g o) -> p g o", o=1).broadcast_to([P, 16, 32]),
                ALU.is_equal))
            dma_step(q, "g", [(pm_d[:, tt * TT:(tt + 1) * TT], pmt[:])])

        for half in range(2):
            dma_step(q, "y", [(cm.wf[:, 0:16 * P], selrep_h[:, half * 16 * P:(half + 1) * 16 * P])])
            q.run("v", lambda v, half=half: v.tensor_copy(
                selrep_s[:, half * 16 * P:(half + 1) * 16 * P], cm.wf[:, 0:16 * P]))
        dma_step(q, "y", [(b1f_s[:], b1f_h)])
        for tt in range(T // TT):
            dma_step(q, "y", [(cm.xbs[:, c * TT:(c + 1) * TT],
                               xn_d[c * P:(c + 1) * P, tt * TT:(tt + 1) * TT]) for c in range(KC)]
                     + [(pmt[:], pm_d[:, tt * TT:(tt + 1) * TT])])
            for mg in range(8):
                dma_step(q, "y", [(cm.wf[:, i * TT:(i + 1) * TT] if False else
                                   cm.wf[:, (i * KC + c) * P:(i * KC + c + 1) * P],
                                   w1a_h[c][:, (mg * 4 + i) * P:(mg * 4 + i + 1) * P])
                                  for i in range(2) for c in range(KC)])
                q.run("v", lambda v: [v.tensor_copy(cm.ot[:, (i * KC + c) * P:(i * KC + c + 1) * P],
                                                    cm.wf[:, (i * KC + c) * P:(i * KC + c + 1) * P])
                                      for i in range(2) for c in range(KC)])
                dma_step(q, "y", [(cm.wf[:, (i * KC + c) * P:(i * KC + c + 1) * P],
                                   w1a_h[c][:, (mg * 4 + 2 + i) * P:(mg * 4 + 3 + i) * P])
                                  for i in range(2) for c in range(KC)])
                q.run("v", lambda v: [v.tensor_copy(cm.ot[:, ((2 + i) * KC + c) * P:((2 + i) * KC + c + 1) * P],
                                                    cm.wf[:, (i * KC + c) * P:(i * KC + c + 1) * P])
                                      for i in range(2) for c in range(KC)])

                def mmh(t, mg=mg):
                    ins = None
                    for i in range(4):
                        pp = [cm.pA, cm.pB, cm.pC, cm.pD][i]
                        for c in range(KC):
                            ins = t.matmul(pp[:], cm.ot[:, (i * KC + c) * P:(i * KC + c + 1) * P],
                                           cm.xbs[:, c * TT:(c + 1) * TT],
                                           start=(c == 0), stop=(c == KC - 1),
                                           skip_group_check=True)
                    return ins
                q.run("t", mmh)
                q.run("v", lambda v, mg=mg: [
                    v.tensor_copy(zp[:, (mg * 4 + i) * TT:(mg * 4 + i + 1) * TT],
                                  [cm.pA, cm.pB, cm.pC, cm.pD][i][:]) for i in range(4)])
            for c2 in range(32):
                q.run("t", lambda t, c2=c2: t.matmul(
                    cm.pA[:], selrep_s[:, c2 * P:(c2 + 1) * P], pmt[:],
                    start=True, stop=True, skip_group_check=True))
                q.run("v", lambda v, c2=c2: v.tensor_mul(
                    pr[:], zp[:, c2 * TT:(c2 + 1) * TT], cm.pA[:]))
                q.run("v", lambda v, c2=c2: v.reduce_sum(
                    mx[:], pr[:].rearrange("p (g t) -> p g t", t=32), axis=AXX))
                q.run("v", lambda v, c2=c2, tt=tt: v.tensor_copy(
                    zz[:, c2 * 64 + tt * 16:c2 * 64 + (tt + 1) * 16], mx[:]))
        def zfin(v):
            ins = None
            for c2 in range(32):
                v.tensor_scalar(zz[:, c2 * 64:(c2 + 1) * 64], zz[:, c2 * 64:(c2 + 1) * 64],
                                b1f_s[:, c2:c2 + 1], None, ALU.add)
                ins = v.tensor_scalar_max(zz[:, c2 * 64:(c2 + 1) * 64],
                                          zz[:, c2 * 64:(c2 + 1) * 64], 0.0)
            return ins
        q.run("v", zfin)

        dma_step(q, "y", [(cm.wf[:, 0:D], b2a_h)])
        q.run("v", lambda v: v.tensor_copy(b2a_s[:], cm.wf[:, 0:D]))
        for tt in range(T // TT):
            dma_step(q, "y", [(pmt[:], pm_d[:, tt * TT:(tt + 1) * TT])]
                     + [(cm.xs[:, c * TT:(c + 1) * TT],
                         xa_d[c * P:(c + 1) * P, tt * TT:(tt + 1) * TT]) for c in range(KC)])
            for c2 in range(32):
                q.run("t", lambda t, c2=c2: t.matmul(
                    cm.pA[:], selrep_s[:, c2 * P:(c2 + 1) * P], pmt[:],
                    start=True, stop=True, skip_group_check=True))
                q.run("v", lambda v, c2=c2, tt=tt: v.tensor_mul(
                    zp[:, c2 * TT:(c2 + 1) * TT],
                    zz[:, c2 * 64 + tt * 16:c2 * 64 + (tt + 1) * 16]
                    .rearrange("p (g o) -> p g o", o=1).broadcast_to([P, 16, 32]),
                    cm.pA[:]))
            for dc in range(KC):
                for ckh in range(2):
                    dma_step(q, "y", [(cm.wf[:, ck * P:(ck + 1) * P],
                                       w2a_h[16 * ckh + ck][:, dc * P:(dc + 1) * P])
                                      for ck in range(16)])
                    q.run("v", lambda v, ckh=ckh: [
                        v.tensor_copy(cm.ot[:, (16 * ckh + ck) * P:(16 * ckh + ck + 1) * P],
                                      cm.wf[:, ck * P:(ck + 1) * P]) for ck in range(16)])

                def mmw2(t, dc=dc):
                    for ck in range(32):
                        t.matmul(pO[:], cm.ot[:, ck * P:(ck + 1) * P], zp[:, ck * TT:(ck + 1) * TT],
                                 start=(ck == 0), stop=False, skip_group_check=True)
                    return t.matmul(pO[:], b2a_s[:, dc * P:(dc + 1) * P], pmt[:],
                                    start=False, stop=True, skip_group_check=True)
                q.run("t", mmw2)
                q.run("v", lambda v, dc=dc: v.tensor_add(
                    cm.xs[:, dc * TT:(dc + 1) * TT], cm.xs[:, dc * TT:(dc + 1) * TT], pO[:]))
            tstore = dma_step(q, "g", [(xout_h[c * P:(c + 1) * P, tt * TT:(tt + 1) * TT],
                                        cm.xs[:, c * TT:(c + 1) * TT]) for c in range(KC)])
        q.run("g", lambda g: None, after=tstore)
    return nc


def prep_L0(inputs):
    Wq = np.asarray(inputs["L0_attn_Wq"], np.float32)
    Wk = np.asarray(inputs["L0_attn_Wk"], np.float32)
    Wv = np.asarray(inputs["L0_attn_Wv"], np.float32)
    out = {}
    wqr = np.empty((D, 2 * D), np.float32)
    bqr = np.empty((2 * D,), np.float32)
    bq = np.asarray(inputs["L0_attn_bq"], np.float32)
    for i in range(16):
        wqr[:, i * 128:i * 128 + 64] = Wq[:, i * 64:(i + 1) * 64]
        wqr[:, i * 128 + 64:i * 128 + 128] = Wq[:, i * 64:(i + 1) * 64]
        bqr[i * 128:i * 128 + 64] = bq[i * 64:(i + 1) * 64]
        bqr[i * 128 + 64:i * 128 + 128] = bq[i * 64:(i + 1) * 64]
    out["wqr"] = np.stack([chunks8(wqr[:, 0:D]), chunks8(wqr[:, D:2 * D])])
    out["bqr"] = bqr.reshape(2, 1, D)
    out["wk"] = chunks8(Wk)
    out["bk"] = np.asarray(inputs["L0_attn_bk"], np.float32).reshape(1, D)
    out["wv"] = chunks8(Wv)
    out["bv"] = np.asarray(inputs["L0_attn_bv"], np.float32).reshape(1, D)
    selred = np.zeros((KC, P, 16), np.float32)
    for c in range(KC):
        for jl in range(2):
            selred[c, jl * 64:(jl + 1) * 64, 2 * c + jl] = 1.0
    out["selred"] = np.ascontiguousarray(selred.transpose(1, 0, 2).reshape(P, KC * 16))
    selmix = np.zeros((16, KC * P), np.float32)
    for c in range(KC):
        for m in range(P):
            selmix[2 * c + m // 64, c * P + m] = 1.0
    out["selmix"] = selmix
    selout = np.zeros((P, 2 * P), np.float32)
    ey = np.eye(64, dtype=np.float32)
    selout[0:64, 0:64] = ey; selout[64:128, 0:64] = ey
    selout[0:64, P + 64:P + 128] = ey; selout[64:128, P + 64:P + 128] = ey
    out["selout"] = selout
    selrep = np.zeros((P, 32 * P), np.float32)
    for c2 in range(32):
        for m in range(P):
            selrep[c2 * 4 + m // 32, c2 * P + m] = 1.0
    out["selrep"] = selrep
    Wc = np.asarray(inputs["L0_ffn_Wc"], np.float32).reshape(D, P)
    out["wc"] = chunks8(Wc)
    out["bc"] = np.asarray(inputs["L0_ffn_bc"], np.float32).reshape(1, P)
    out["tie"] = np.tile(np.linspace(0.0, 1e-6, 32, dtype=np.float32), T // 32).reshape(1, T)
    W1 = np.asarray(inputs["L0_ffn_W1"], np.float32)
    w1a = W1.transpose(2, 0, 1, 3).reshape(D, 4 * D)
    out["w1a"] = chunks8(w1a)
    b1 = np.asarray(inputs["L0_ffn_b1"], np.float32).reshape(P, 32)
    b1f = b1.reshape(4 * D)[np.arange(P)[:, None] + np.arange(32)[None, :] * P]
    out["b1f"] = np.ascontiguousarray(b1f.astype(np.float32))
    W2 = np.asarray(inputs["L0_ffn_W2"], np.float32)
    w2a = W2.transpose(0, 1, 3, 2).reshape(4 * D, D)
    out["w2a"] = np.ascontiguousarray(w2a.reshape(4 * KC, P, D))
    b2 = np.asarray(inputs["L0_ffn_b2"], np.float32).reshape(P, D)
    out["b2a"] = b2
    return out


def run_L0(x_fm, inputs):
    w = prep_L0(inputs)
    nc = build_L0()
    in_maps = []
    for c in range(NCORES):
        b, h = divmod(c, 2)
        in_maps.append(dict(w, xb=np.ascontiguousarray(x_fm[b][:, h * T:(h + 1) * T])))
    res = run_bass_kernel_spmd(nc, in_maps, core_ids=list(range(NCORES))).results
    x_new = np.empty_like(x_fm)
    _scatter_out(res, x_new)
    return x_new


def build_L2a():
    nc = bass.Bass("TRN2", target_bir_lowering=False)
    dt = nc.dram_tensor
    xb_h = dt("xb", [D, TK], F32, kind="ExternalInput").ap()
    wq_h = dt("wq", [KC, P, D], F32, kind="ExternalInput").ap()
    wk_h = dt("wk", [KC, P, D], F32, kind="ExternalInput").ap()
    wv_h = dt("wv", [KC, P, D], F32, kind="ExternalInput").ap()
    wo_h = dt("wo", [KC, P, D], F32, kind="ExternalInput").ap()
    bo_h = dt("bo", [1, D], F32, kind="ExternalInput").ap()
    gamb_h = dt("gamb", [P, 16], F32, kind="ExternalInput").ap()
    gqc_h = dt("gqc", [P, KC * 16], F32, kind="ExternalInput").ap()
    gkc_h = dt("gkc", [P, KC * 16], F32, kind="ExternalInput").ap()
    selred_h = dt("selred", [P, KC * 16], F32, kind="ExternalInput").ap()
    selh_h = dt("selh", [16, 16 * 64], F32, kind="ExternalInput").ap()
    gw_h = dt("gw", [KC, P, 64], F32, kind="ExternalInput").ap()
    gbias_h = dt("gbias", [1, 64], F32, kind="ExternalInput").ap()
    ltri_h = dt("ltri", [P, P], F32, kind="ExternalInput").ap()
    iota_h = dt("iota", [P, 64], F32, kind="ExternalInput").ap()
    id_h = dt("idm", [P, P], F32, kind="ExternalInput").ap()
    xaout_h = dt("xaout", [D, T], F32, kind="ExternalOutput").ap()
    xnout_h = dt("xnout", [D, T], BF16, kind="ExternalOutput").ap()
    ek_h = dt("ek", [T, 8], F32, kind="ExternalOutput").ap()
    posk_h = dt("posk", [T, 8], F32, kind="ExternalOutput").ap()
    wk8_h = dt("wk8", [T, 8], F32, kind="ExternalOutput").ap()
    cnt_h = dt("cnt", [1, 64], F32, kind="ExternalOutput").ap()

    xn_d = dt("xn_d", [D, TK], BF16).ap()
    xn2_d = dt("xn2_d", [D, TK], BF16).ap()
    q2_d = dt("q2_d", [D, T], BF16).ap()
    k2_d = dt("k2_d", [D, TK], BF16).ap()
    vk_d = dt("vk_d", [TK, D], BF16).ap()
    at2_d = dt("at2_d", [D, T], BF16).ap()

    st = ExitStack()
    q = Seq(nc, st)
    with st:
        cm = Common(q)
        wbf = q.sb("wbf", P, KC * D, BF16)
        et = q.sb("et", P, 32 * TT, BF16)
        kh = q.sb("kh", 64, TK, BF16)
        vh = q.sb("vh", P, 32 * 64, BF16)
        qh = q.sb("qh", 64, TT, BF16)
        selred_s = q.sb("selred_s", P, KC * 16, BF16)
        gqc_s = q.sb("gqc_s", P, KC * 16, BF16)
        gkc_s = q.sb("gkc_s", P, KC * 16, BF16)
        selh_s = q.sb("selh_s", 16, 16 * 64, BF16)
        gamb_s = q.sb("gamb_s", P, 16, F32)
        gqf = q.sb("gqf", 16, T, BF16)
        gkf = q.sb("gkf", 16, TK, BF16)
        gktm = q.sb("gktm", P, 32 * 16, BF16)
        nrm = q.sb("nrm", 16, TT, F32)
        nrmb = q.sb("nrmb", 16, TT, BF16)
        gates = q.sb("gates", P, 64, F32)
        sfc = q.sb("sfc", P, 64, F32)
        gs8 = q.sb("gs8", P, 8, F32)
        smask = q.sb("smask", P, 64, F32)
        selm = q.sb("selm", P, 64, F32)
        wt = q.sb("wt", P, 64, F32)
        ek_s = q.sb("ek_s", P, 8, F32)
        posk_s = q.sb("posk_s", P, 8, F32)
        wk8_s = q.sb("wk8_s", P, 8, F32)
        posl = q.sb("posl", P, 64, F32)
        runr = q.sb("runr", 1, 64, F32)
        gbt = q.sb("gbt", P, 64, F32)
        iot = q.sb("iot", P, 64, F32)
        ltri_s = q.sb("ltri_s", P, P, F32)
        idb_s = q.sb("idb_s", P, P, BF16)
        scr = q.sb("scr", P, 64, F32)
        mrow = q.sb("mrow", P, 2, F32)
        c405 = q.sb("c405", P, 1, F32)
        pF = q.ps("pF", P, TT)
        pE = q.ps("pE", P, TT, BF16)

        ln_to_dram(q, cm, xb_h, xn_d, TK)
        dma_step(q, "g", [(xnout_h[:], xn_d[:, 0:T])])

        dma_step(q, "y", [(gamb_s[:], gamb_h)])
        nt = TK // TT
        for j in range(nt):
            dma_step(q, "y", [(cm.xbs[:, c * TT:(c + 1) * TT],
                               xn_d[c * P:(c + 1) * P, j * TT:(j + 1) * TT]) for c in range(KC)])
            q.run("s", lambda s: s.activation(cm.sq[:], cm.xbs[:], AF.Square))

            def mms(t):
                for c in range(KC):
                    t.matmul(cm.pA[0:1, :], cm.onesb[:, 0:1], cm.xbs[:, c * TT:(c + 1) * TT],
                             start=(c == 0), stop=(c == KC - 1), skip_group_check=True)
                ins = None
                for c in range(KC):
                    ins = t.matmul(cm.pB[0:1, :], cm.onesb[:, 0:1], cm.sq[:, c * TT:(c + 1) * TT],
                                   start=(c == 0), stop=(c == KC - 1), skip_group_check=True)
                return ins
            q.run("t", mms)
            q.run("v", lambda v, j=j: [
                v.tensor_copy(cm.rows[0:1, j * TT:(j + 1) * TT], cm.pA[0:1, :]),
                v.tensor_copy(cm.rows[0:1, TK + j * TT:TK + (j + 1) * TT], cm.pB[0:1, :])])

        def rowmath(v):
            v.tensor_scalar_mul(cm.rows[0:1, 0:TK], cm.rows[0:1, 0:TK], 1.0 / D)
            v.tensor_scalar_mul(cm.rows[0:1, TK:2 * TK], cm.rows[0:1, TK:2 * TK], 1.0 / D)
            v.tensor_tensor(cm.xs[0:1, 0:TK], cm.rows[0:1, 0:TK], cm.rows[0:1, 0:TK], ALU.mult)
            v.tensor_sub(cm.rows[0:1, TK:2 * TK], cm.rows[0:1, TK:2 * TK], cm.xs[0:1, 0:TK])
            return v.tensor_scalar_add(cm.rows[0:1, TK:2 * TK], cm.rows[0:1, TK:2 * TK], LN_EPS)
        q.run("v", rowmath)
        q.run("s", lambda s: s.activation(cm.rows[0:1, TK:2 * TK], cm.rows[0:1, TK:2 * TK], AF.Sqrt))
        q.run("v", lambda v: v.reciprocal(cm.rows[0:1, TK:2 * TK], cm.rows[0:1, TK:2 * TK]))
        for j in range(nt):
            dma_step(q, "y", [(cm.xbs[:, c * TT:(c + 1) * TT],
                               xn_d[c * P:(c + 1) * P, j * TT:(j + 1) * TT]) for c in range(KC)])

            def bcasts(t, j=j):
                t.matmul(cm.pA[:], cm.onesr[0:1, 0:P], cm.rows[0:1, j * TT:(j + 1) * TT],
                         start=True, stop=True, skip_group_check=True)
                return t.matmul(cm.pB[:], cm.onesr[0:1, 0:P],
                                cm.rows[0:1, TK + j * TT:TK + (j + 1) * TT],
                                start=True, stop=True, skip_group_check=True)
            q.run("t", bcasts)

            def norm(v):
                ins = None
                for c in range(KC):
                    sl = slice(c * TT, (c + 1) * TT)
                    v.tensor_sub(cm.xs[:, sl], cm.xbs[:, sl], cm.pA[:])
                    v.tensor_mul(cm.xs[:, sl], cm.xs[:, sl], cm.pB[:])
                    ins = v.tensor_scalar(cm.xbs[:, sl], cm.xs[:, sl],
                                          gamb_s[:, c:c + 1], gamb_s[:, 8 + c:9 + c],
                                          ALU.mult, ALU.add)
                return ins
            q.run("v", norm)
            dma_step(q, "g", [(xn2_d[c * P:(c + 1) * P, j * TT:(j + 1) * TT],
                               cm.xbs[:, c * TT:(c + 1) * TT]) for c in range(KC)])

        q.run("v", lambda v: v.memset(cm.bias[0:1, 0:4352], 0.0))
        dma_step(q, "y", [(cm.wf[:, 0:KC * 16], selred_h),
                          (cm.wf[:, 256:256 + KC * 16], gqc_h),
                          (cm.wf[:, 512:512 + KC * 16], gkc_h),
                          (cm.wf[0:16, 1024:1024 + 16 * 64], selh_h)])
        q.run("v", lambda v: [v.tensor_copy(selred_s[:], cm.wf[:, 0:KC * 16]),
                              v.tensor_copy(gqc_s[:], cm.wf[:, 256:256 + KC * 16]),
                              v.tensor_copy(gkc_s[:], cm.wf[:, 512:512 + KC * 16]),
                              v.tensor_copy(selh_s[:], cm.wf[0:16, 1024:1024 + 16 * 64])])

        def norml2(src_d, dst_d, ncols_, gfc, gf_s):
            for j in range(ncols_ // TT):
                dma_step(q, "y", [(cm.xbs[:, c * TT:(c + 1) * TT],
                                   src_d[c * P:(c + 1) * P, j * TT:(j + 1) * TT])
                                  for c in range(KC)])
                q.run("s", lambda s: s.activation(cm.sq[:], cm.xbs[:], AF.Square))

                def mmn(t):
                    ins = None
                    for c in range(KC):
                        ins = t.matmul(cm.pC[0:16, :], selred_s[:, c * 16:(c + 1) * 16], cm.sq[:, c * TT:(c + 1) * TT],
                                       start=(c == 0), stop=(c == KC - 1), skip_group_check=True)
                    return ins
                q.run("t", mmn)
                q.run("v", lambda v: v.tensor_copy(nrm[:], cm.pC[0:16, :]))
                q.run("s", lambda s: s.activation(nrm[:], nrm[:], AF.Sqrt))
                q.run("v", lambda v: v.tensor_scalar_max(nrm[:], nrm[:], 1e-12))
                q.run("v", lambda v: v.reciprocal(nrm[:], nrm[:]))
                q.run("v", lambda v: v.tensor_copy(nrmb[:], nrm[:]))
                for c in range(KC):
                    q.run("t", lambda t, c=c: t.matmul(
                        cm.pD[:], selmixview(selh_s, c), nrmb[:],
                        start=True, stop=True, skip_group_check=True))
                    q.run("v", lambda v, c=c: v.tensor_mul(
                        cm.ot[:, c * TT:(c + 1) * TT], cm.xbs[:, c * TT:(c + 1) * TT], cm.pD[:]))
                dma_step(q, "g", [(dst_d[c * P:(c + 1) * P, j * TT:(j + 1) * TT],
                                   cm.ot[:, c * TT:(c + 1) * TT]) for c in range(KC)])
                def mmg(t, j=j):
                    ins = None
                    for c in range(KC):
                        ins = t.matmul(cm.pC[0:16, :], gfc[:, c * 16:(c + 1) * 16],
                                       cm.ot[:, c * TT:(c + 1) * TT],
                                       start=(c == 0), stop=(c == KC - 1), skip_group_check=True)
                    return ins
                q.run("t", mmg)
                if gf_s is gqf:
                    q.run("s", lambda s, j=j: s.activation(gf_s[:, j * TT:(j + 1) * TT],
                                                           cm.pC[0:16, :], AF.Sigmoid))
                    q.run("v", lambda v, j=j: v.tensor_scalar(
                        gf_s[:, j * TT:(j + 1) * TT], gf_s[:, j * TT:(j + 1) * TT],
                        0.375, 0.125, ALU.mult, ALU.add))
                else:
                    q.run("s", lambda s, j=j: s.activation(gf_s[:, j * TT:(j + 1) * TT],
                                                           cm.pC[0:16, :], AF.Sigmoid))
                    q.run("v", lambda v, j=j: v.tensor_scalar(
                        gf_s[:, j * TT:(j + 1) * TT], gf_s[:, j * TT:(j + 1) * TT],
                        0.75, 0.25, ALU.mult, ALU.add))

        def selmixview(sel_s, c):
            return sel_s[:, (2 * c) * 64:(2 * c + 2) * 64]

        load_w(q, cm, [wq_h[c] for c in range(KC)], wbf, D)
        proj_fm(q, cm, wbf, 0, xn2_d, q2_d, D, T, xoff=0)
        norml2(q2_d, q2_d, T, gqc_s, gqf)
        load_w(q, cm, [wk_h[c] for c in range(KC)], wbf, D)
        proj_fm(q, cm, wbf, 0, xn2_d, k2_d, D, TK, xoff=0)
        norml2(k2_d, k2_d, TK, gkc_s, gkf)
        dma_step(q, "y", [(cm.xs[0:P, 0:P], id_h)])
        q.run("v", lambda v: v.tensor_copy(idb_s[:], cm.xs[0:P, 0:P]))
        for kt_ in range(TK // P):
            q.run("t", lambda t, kt_=kt_: t.transpose(
                pE[0:P, 0:16], gkf[:, kt_ * P:(kt_ + 1) * P], idb_s[0:16, 0:16]))
            q.run("v", lambda v, kt_=kt_: v.tensor_copy(gktm[:, kt_ * 16:(kt_ + 1) * 16],
                                                        pE[0:P, 0:16]))
        load_w(q, cm, [wv_h[c] for c in range(KC)], wbf, D)
        proj_tm(q, cm, wbf, 0, xn_d, vk_d, TK, fdim=D)
        for kt_ in range(TK // P):
            dma_step(q, "y", [(et[:, 0:D], vk_d[kt_ * P:(kt_ + 1) * P, :])])
            q.run("v", lambda v, kt_=kt_: v.tensor_mul(
                et[:, 0:D].rearrange("p (h o) -> p h o", h=16),
                et[:, 0:D].rearrange("p (h o) -> p h o", h=16),
                gktm[:, kt_ * 16:(kt_ + 1) * 16].rearrange("p (h o) -> p h o", o=1)
                .broadcast_to([P, 16, 64])))
            dma_step(q, "g", [(vk_d[kt_ * P:(kt_ + 1) * P, :], et[:, 0:D])])

        q.run("v", lambda v: v.memset(c405[:], -4.05))
        for hh in range(16):
            dma_step(q, "y", [(kh[:], k2_d[hh * 64:(hh + 1) * 64, :])]
                     + [(vh[:, kc * 64:(kc + 1) * 64],
                         vk_d[kc * P:(kc + 1) * P, hh * 64:(hh + 1) * 64]) for kc in range(32)])
            for qt in range(T // TT):
                dma_step(q, "y", [(qh[:], q2_d[hh * 64:(hh + 1) * 64, qt * TT:(qt + 1) * TT])])
                for kc0 in range(0, 32, 2):
                    def mmsc(t, kc0=kc0):
                        t.matmul(cm.pA[:], kh[:, kc0 * P:(kc0 + 1) * P], qh[:],
                                 start=True, stop=True, skip_group_check=True)
                        return t.matmul(cm.pB[:], kh[:, (kc0 + 1) * P:(kc0 + 2) * P], qh[:],
                                        start=True, stop=True, skip_group_check=True)
                    q.run("t", mmsc)
                    q.run("s", lambda s, kc0=kc0: [
                        s.activation(et[:, kc0 * TT:(kc0 + 1) * TT], cm.pA[:], AF.Tanh,
                                     scale=15.0, bias=c405[:, 0:1]),
                        s.activation(et[:, (kc0 + 1) * TT:(kc0 + 2) * TT], cm.pB[:], AF.Tanh,
                                     scale=15.0, bias=c405[:, 0:1])])

                def mmfu(t):
                    ins = None
                    for kc in range(32):
                        t.matmul(pF[0:64, :], vh[:, kc * 64:(kc + 1) * 64],
                                 et[:, kc * TT:(kc + 1) * TT],
                                 start=(kc == 0), stop=False, skip_group_check=True)
                        ins = t.matmul(pF[0:64, :], vh[:, kc * 64:(kc + 1) * 64],
                                       cm.onesb[:, 0:1].broadcast_to([P, TT]),
                                       start=False, stop=(kc == 31), skip_group_check=True)
                    return ins
                q.run("t", mmfu)
                q.run("t", lambda t, hh=hh, qt=qt: t.matmul(
                    cm.pD[0:64, :], selh_s[:, hh * 64:(hh + 1) * 64],
                    gqf[:, qt * TT:(qt + 1) * TT], start=True, stop=True,
                    skip_group_check=True))
                q.run("v", lambda v: v.tensor_copy(cm.ot[0:64, 0:TT], pF[0:64, :]))
                q.run("v", lambda v: v.tensor_mul(cm.ot[0:64, 0:TT], cm.ot[0:64, 0:TT],
                                                  cm.pD[0:64, :]))
                dma_step(q, "g", [(at2_d[hh * 64:(hh + 1) * 64, qt * TT:(qt + 1) * TT],
                                   cm.ot[0:64, 0:TT])])

        load_w(q, cm, [wo_h[c] for c in range(KC)], wbf, D)
        dma_step(q, "y", [(cm.bias[0:1, 0:D], bo_h)])
        for j in range(T // TT):
            dma_step(q, "y", [(cm.xbs[:, c * TT:(c + 1) * TT],
                               at2_d[c * P:(c + 1) * P, j * TT:(j + 1) * TT]) for c in range(KC)]
                     + [(cm.xs[:, c * TT:(c + 1) * TT],
                         xb_h[c * P:(c + 1) * P, j * TT:(j + 1) * TT]) for c in range(KC)])
            for g0 in (0, 4):
                def mmo(t, g0=g0):
                    ins = None
                    for i in range(4):
                        dc = g0 + i
                        pp = [cm.pA, cm.pB, cm.pC, cm.pD][i]
                        for c in range(KC):
                            t.matmul(pp[:], wbf[:, c * D + dc * P:c * D + (dc + 1) * P],
                                     cm.xbs[:, c * TT:(c + 1) * TT],
                                     start=(c == 0), stop=False, skip_group_check=True)
                        ins = t.matmul(pp[:], cm.bias[0:1, dc * P:(dc + 1) * P],
                                       cm.onesr[0:1, :], start=False, stop=True,
                                       skip_group_check=True)
                    return ins
                q.run("t", mmo)
                q.run("v", lambda v, g0=g0: [
                    v.tensor_add(cm.xs[:, (g0 + i) * TT:(g0 + i + 1) * TT],
                                 cm.xs[:, (g0 + i) * TT:(g0 + i + 1) * TT],
                                 [cm.pA, cm.pB, cm.pC, cm.pD][i][:]) for i in range(4)])
            dma_step(q, "g", [(xaout_h[c * P:(c + 1) * P, j * TT:(j + 1) * TT],
                               cm.xs[:, c * TT:(c + 1) * TT]) for c in range(KC)])

        load_w(q, cm, [gw_h[c] for c in range(KC)], wbf, 64)
        dma_step(q, "y", [(cm.bias[0:1, 0:64], gbias_h), (iot[:], iota_h),
                          (ltri_s[:], ltri_h)])
        q.run("v", lambda v: v.memset(runr[:], 0.0))
        q.run("t", lambda t: t.matmul(cm.pA[:, 0:64], cm.onesr[0:1, 0:P], cm.bias[0:1, 0:64],
                                      start=True, stop=True, skip_group_check=True))
        q.run("v", lambda v: v.tensor_copy(gbt[:], cm.pA[:, 0:64]))
        for jt in range(T // P):
            dma_step(q, "y", [(cm.xbs[:, c * P:(c + 1) * P],
                               xn_d[c * P:(c + 1) * P, jt * P:(jt + 1) * P]) for c in range(KC)])

            def mmgl(t):
                ins = None
                for c in range(KC):
                    ins = t.matmul(cm.pA[:, 0:64], cm.xbs[:, c * P:(c + 1) * P],
                                   wbf[:, c * 64:(c + 1) * 64],
                                   start=(c == 0), stop=(c == KC - 1), skip_group_check=True)
                return ins
            q.run("t", mmgl)
            q.run("s", lambda s: s.activation(gates[:], cm.pA[:, 0:64], AF.Sigmoid))
            q.run("v", lambda v: v.tensor_add(sfc[:], gates[:], gbt[:]))
            q.run("v", lambda v: v.reduce_max(gs8[:], sfc[:].rearrange("p (g e) -> p g e", e=8),
                                              axis=AXX))
            q.run("v", lambda v: v.tensor_tensor(
                smask[:], sfc[:], gs8[:].rearrange("p (g o) -> p g o", o=1).broadcast_to([P, 8, 8]),
                ALU.is_equal))
            q.run("v", lambda v: [v.tensor_scalar_mul(scr[:], smask[:], -1e30),
                                  v.tensor_add(scr[:], scr[:], sfc[:])])
            q.run("v", lambda v: v.reduce_max(wt[:, 0:8], scr[:].rearrange("p (g e) -> p g e", e=8),
                                              axis=AXX))
            q.run("v", lambda v: v.tensor_add(gs8[:], gs8[:], wt[:, 0:8]))
            q.run("v", lambda v: v.memset(smask[:], 0.0))
            for it_ in range(4):
                q.run("v", lambda v: v.reduce_max(wt[:, 8:9], gs8[:], axis=AXX))
                q.run("v", lambda v: v.tensor_tensor(wt[:, 16:24], gs8[:],
                                                     wt[:, 8:9].broadcast_to([P, 8]), ALU.is_equal))
                q.run("v", lambda v: [
                    v.tensor_add(smask[:].rearrange("p (g e) -> p g e", e=8)
                                 if False else smask[:], smask[:], wt[:, 16:24]
                                 .rearrange("p (g o) -> p g o", o=1).broadcast_to([P, 8, 8])),
                    v.tensor_scalar_mul(wt[:, 24:32], wt[:, 16:24], -1e30)])
                q.run("v", lambda v: v.tensor_add(gs8[:], gs8[:], wt[:, 24:32]))
            q.run("v", lambda v: [v.tensor_scalar(scr[:], smask[:], 1e30, -1e30, ALU.mult, ALU.add),
                                  v.tensor_scalar_min(scr[:], scr[:], 0.0)])
            q.run("v", lambda v: v.tensor_add(scr[:], scr[:], sfc[:]))
            q.run("v", lambda v: v.memset(selm[:], 0.0))
            for k8 in range(8):
                q.run("v", lambda v: v.reduce_max(wt[:, 8:9], scr[:], axis=AXX))
                q.run("v", lambda v: v.tensor_tensor(smask[:], scr[:],
                                                     wt[:, 8:9].broadcast_to([P, 64]), ALU.is_equal))
                q.run("v", lambda v: v.tensor_add(selm[:], selm[:], smask[:]))
                q.run("v", lambda v: v.tensor_mul(wt[:, 32:96] if False else sfc[:],
                                                  smask[:], iot[:]))
                q.run("v", lambda v, k8=k8: v.reduce_sum(ek_s[:, k8:k8 + 1], sfc[:], axis=AXX))
                q.run("v", lambda v: [v.tensor_scalar_mul(sfc[:], smask[:], -1e30),
                                      v.tensor_add(scr[:], scr[:], sfc[:])])
            q.run("v", lambda v: v.tensor_mul(wt[:], gates[:], selm[:]))
            q.run("v", lambda v: v.reduce_sum(wt[:, 8:9] if False else ek_s[:, 0:1], wt[:], axis=AXX)
                  if False else v.reduce_sum(scr[:, 0:1], wt[:], axis=AXX))
            q.run("v", lambda v: v.tensor_scalar_add(scr[:, 0:1], scr[:, 0:1], 1e-20))
            q.run("v", lambda v: v.reciprocal(scr[:, 1:2], scr[:, 0:1]))
            q.run("v", lambda v: v.tensor_scalar(wt[:], wt[:], scr[:, 1:2], 2.5, ALU.mult, ALU.mult))
            def mmpos(t):
                t.matmul(cm.pB[:, 0:64], ltri_s[:], selm[:],
                         start=True, stop=False, skip_group_check=True)
                return t.matmul(cm.pB[:, 0:64], cm.onesr[0:1, 0:P], runr[0:1, :],
                                start=False, stop=True, skip_group_check=True)
            q.run("t", mmpos)
            q.run("v", lambda v: v.tensor_copy(posl[:], cm.pB[:, 0:64]))
            def mmcnt(t):
                return t.matmul(cm.pC[0:1, 0:64], cm.ones[:, 0:1], selm[:],
                                start=True, stop=True, skip_group_check=True)
            q.run("t", mmcnt)
            q.run("v", lambda v: v.tensor_add(runr[0:1, :], runr[0:1, :], cm.pC[0:1, 0:64]))
            for k8 in range(8):
                q.run("v", lambda v, k8=k8: v.tensor_tensor(
                    smask[:], iot[:], ek_s[:, k8:k8 + 1].broadcast_to([P, 64]), ALU.is_equal))
                q.run("v", lambda v, k8=k8: v.tensor_mul(sfc[:], smask[:], posl[:]))
                q.run("v", lambda v, k8=k8: v.reduce_sum(posk_s[:, k8:k8 + 1], sfc[:], axis=AXX))
                q.run("v", lambda v, k8=k8: v.tensor_mul(sfc[:], smask[:], wt[:]))
                q.run("v", lambda v, k8=k8: v.reduce_sum(wk8_s[:, k8:k8 + 1], sfc[:], axis=AXX))
            dma_step(q, "g", [(ek_h[jt * P:(jt + 1) * P, :], ek_s[:]),
                              (posk_h[jt * P:(jt + 1) * P, :], posk_s[:]),
                              (wk8_h[jt * P:(jt + 1) * P, :], wk8_s[:])])
        q.run("v", lambda v: v.tensor_copy(cm.rows[0:1, 0:64], runr[0:1, :]))
        tstore = dma_step(q, "g", [(cnt_h[:], cm.rows[0:1, 0:64])])
        q.run("g", lambda g: None, after=tstore)
    return nc


CAPL = 512
ZROW = 64 * CAPL
TROW = 64 * CAPL + 1


def build_L2b(debug=False):
    nc = bass.Bass("TRN2", target_bir_lowering=False)
    dt = nc.dram_tensor
    xn_h = dt("xn", [D, T], BF16, kind="ExternalInput").ap()
    xa_h = dt("xa", [D, T], F32, kind="ExternalInput").ap()
    ek_h = dt("ek", [T, 8], F32, kind="ExternalInput").ap()
    posk_h = dt("posk", [T, 8], F32, kind="ExternalInput").ap()
    wk8_h = dt("wk8", [T, 8], F32, kind="ExternalInput").ap()
    cnts_h = dt("cnts", [8, 64], F32, kind="ExternalInput").ap()
    mask_h = dt("mask", [8, 1], F32, kind="ExternalInput").ap()
    iota_h = dt("iota", [P, 64], F32, kind="ExternalInput").ap()
    id_h = dt("idm", [P, P], F32, kind="ExternalInput").ap()
    wg_h = dt("wg", [64, KC, P, 512], F32, kind="ExternalInput").ap()
    wu_h = dt("wu", [64, KC, P, 512], F32, kind="ExternalInput").ap()
    wd_h = dt("wd", [64, 4, P, D], F32, kind="ExternalInput").ap()
    xout_h = dt("xout", [D, T], F32, kind="ExternalOutput").ap()

    buf_d = dt("buf_d", [64 * CAPL + 2, D], BF16).ap()
    y_d = dt("y_d", [64 * CAPL + 2, D], BF16).ap()
    if debug:
        dbuf_h = dt("dbuf", [2 * CAPL, D], F32, kind="ExternalOutput").ap()
        dy_h = dt("dy", [2 * CAPL, D], F32, kind="ExternalOutput").ap()
        dwf_h = dt("dwf", [P, 16 * 8], F32, kind="ExternalOutput").ap()
        dacc_h = dt("dacc", [P, D], F32, kind="ExternalOutput").ap()

    st = ExitStack()
    q = Seq(nc, st)
    with st:
        cm = Common(q)
        wgb = q.sb("wgb", P, KC * 512, BF16)
        wub = q.sb("wub", P, KC * 512, BF16)
        bufT = q.sb("bufT", P, KC * 512, BF16)
        hp = q.sb("hp", P, 4 * 512, BF16)
        sgt = q.sb("sgt", P, 4 * 512, BF16)
        utt = q.sb("utt", P, 4 * 512, BF16)
        xtm = q.sb("xtm", P, D, BF16)
        gt = q.sb("gt", P, D, BF16)
        acc = q.sb("acc", P, D, F32)
        ekt = q.sb("ekt", P, 8, F32)
        poskt = q.sb("poskt", P, 8, F32)
        wfin = q.sb("wfin", P, 16 * 8, F32)
        iot = q.sb("iot", P, 64, F32)
        offt = q.sb("offt", P, 64, F32)
        offr = q.sb("offr", 1, 64, F32)
        maskc = q.sb("maskc", 8, 1, F32)
        cnts_s = q.sb("cnts_s", 8, 64, F32)
        scr = q.sb("scr", P, 64, F32)
        scr8 = q.sb("scr8", P, 32, F32)
        idxs = q.sb("idxs", P, 2, I32)
        id_s = q.sb("id_s", P, P, F32)
        zrow = q.sb("zrow", 1, D, BF16)
        pT = q.ps("pT", P, P)

        dma_step(q, "y", [(cnts_s[:], cnts_h), (maskc[:], mask_h), (iot[:], iota_h),
                          (id_s[:], id_h)])
        q.run("t", lambda t: t.matmul(cm.pA[0:1, 0:64], maskc[:], cnts_s[:],
                                      start=True, stop=True, skip_group_check=True))
        q.run("v", lambda v: [v.tensor_copy(offr[:], cm.pA[0:1, 0:64]),
                              v.memset(zrow[:], 0.0),
                              v.memset(cm.xbs[:], 0.0)])
        dma_step(q, "g", [(y_d[ZROW:ZROW + 1, :], zrow[:])])
        NR = 64 * CAPL + 2
        for blk0 in range(0, NR, 32 * P):
            n = min(32 * P, NR - blk0)
            dma_step(q, "g", [(buf_d[blk0 + r * P:blk0 + min((r + 1) * P, blk0 + n - r * P + r * P), :]
                               if False else buf_d[blk0 + r * P:blk0 + (r + 1) * P, :],
                               cm.xbs[:, 0:D])
                              for r in range((n + P - 1) // P) if blk0 + r * P < NR
                              for _ in [0] if min(P, NR - blk0 - r * P) == P])
        rem = NR % P
        if rem:
            dma_step(q, "g", [(buf_d[NR - rem:NR, :], cm.xbs[0:rem, 0:D])])
        for blk0 in range(0, NR - rem, 32 * P):
            n = min(32 * P, NR - rem - blk0)
            dma_step(q, "g", [(y_d[blk0 + r * P:blk0 + (r + 1) * P, :], cm.xbs[:, 0:D])
                              for r in range(n // P)])
        if rem:
            dma_step(q, "g", [(y_d[NR - rem:NR, :], cm.xbs[0:rem, 0:D])])
        q.run("t", lambda t: t.matmul(cm.pB[:, 0:64], cm.onesr[0:1, 0:P], offr[:],
                                      start=True, stop=True, skip_group_check=True))
        q.run("v", lambda v: v.tensor_copy(offt[:], cm.pB[:, 0:64]))

        for jt in range(T // P):
            dma_step(q, "y", [(ekt[:], ek_h[jt * P:(jt + 1) * P, :]),
                              (poskt[:], posk_h[jt * P:(jt + 1) * P, :]),
                              (scr8[:, 0:8], wk8_h[jt * P:(jt + 1) * P, :])])
            q.run("y", lambda y, jt=jt: y.dma_start(
                out=xtm[:], in_=xn_h[:, jt * P:(jt + 1) * P], transpose=True), inc=16)
            for k8 in range(8):
                q.run("v", lambda v, k8=k8: v.tensor_tensor(
                    scr[:], iot[:], ekt[:, k8:k8 + 1].broadcast_to([P, 64]), ALU.is_equal))
                q.run("v", lambda v: v.tensor_mul(scr[:], scr[:], offt[:]))
                q.run("v", lambda v, k8=k8: v.reduce_sum(scr8[:, 8 + k8:9 + k8], scr[:], axis=AXX))
            q.run("v", lambda v: v.tensor_add(scr8[:, 16:24], poskt[:], scr8[:, 8:16]))
            q.run("v", lambda v: v.tensor_scalar(scr8[:, 16:24], scr8[:, 16:24], 1024.0, None,
                                                 ALU.is_lt))
            q.run("v", lambda v: v.tensor_scalar(scr8[:, 24:32], poskt[:], float(CAPL), None,
                                                 ALU.is_lt))
            q.run("v", lambda v: v.tensor_mul(scr8[:, 16:24], scr8[:, 16:24], scr8[:, 24:32]))
            q.run("v", lambda v, jt=jt: v.tensor_mul(wfin[:, jt * 8:(jt + 1) * 8],
                                                     scr8[:, 0:8], scr8[:, 16:24]))
            q.run("v", lambda v: v.tensor_scalar(scr8[:, 24:32], ekt[:], float(CAPL), None,
                                                 ALU.mult))
            q.run("v", lambda v: v.tensor_add(scr8[:, 24:32], scr8[:, 24:32], poskt[:]))
            for k8 in range(8):
                q.run("v", lambda v, k8=k8: [
                    v.tensor_mul(scr[:, 0:1], scr8[:, 24 + k8:25 + k8], scr8[:, 16 + k8:17 + k8]),
                    v.tensor_scalar(scr[:, 1:2], scr8[:, 16 + k8:17 + k8], -float(TROW),
                                    float(TROW), ALU.mult, ALU.add),
                    v.tensor_add(scr[:, 0:1], scr[:, 0:1], scr[:, 1:2])])
                q.run("v", lambda v: v.tensor_copy(idxs[:, 0:1], scr[:, 0:1]))
                q.run("g", lambda g: g.indirect_dma_start(
                    buf_d[:], bass.IndirectOffsetOnAxis(idxs[:, 0:1], 0), xtm[:], None),
                    inc=16)

        for e in range(64):
            load_w(q, cm, [wg_h[e, c] for c in range(KC)], wgb, 512)
            load_w(q, cm, [wu_h[e, c] for c in range(KC)], wub, 512)
            for c in range(KC):
                q.run("y", lambda y, e=e, c=c: y.dma_start(
                    out=bufT[:, c * 512:(c + 1) * 512],
                    in_=buf_d[e * CAPL:(e + 1) * CAPL, c * P:(c + 1) * P], transpose=True),
                    inc=16)
            for g0 in range(4):
                def mmg(t, g0=g0):
                    for c in range(KC):
                        t.matmul(cm.pA[:], wgb[:, c * 512 + g0 * P:c * 512 + (g0 + 1) * P],
                                 bufT[:, c * 512:(c + 1) * 512],
                                 start=(c == 0), stop=(c == KC - 1), skip_group_check=True)
                    ins = None
                    for c in range(KC):
                        ins = t.matmul(cm.pB[:], wub[:, c * 512 + g0 * P:c * 512 + (g0 + 1) * P],
                                       bufT[:, c * 512:(c + 1) * 512],
                                       start=(c == 0), stop=(c == KC - 1), skip_group_check=True)
                    return ins
                q.run("t", mmg)
                q.run("s", lambda s, g0=g0: s.activation(sgt[:, g0 * 512:(g0 + 1) * 512],
                                                         cm.pA[:], AF.Silu))
                q.run("v", lambda v, g0=g0: v.tensor_copy(utt[:, g0 * 512:(g0 + 1) * 512],
                                                          cm.pB[:]))
                q.run("v", lambda v, g0=g0: v.tensor_mul(hp[:, g0 * 512:(g0 + 1) * 512],
                                                         sgt[:, g0 * 512:(g0 + 1) * 512],
                                                         utt[:, g0 * 512:(g0 + 1) * 512]))
            for ss in range(4):
                for dh in range(2):
                    dma_step(q, "y", [(cm.wf[:, ic * TT:(ic + 1) * TT],
                                       wd_h[e, ic][:, dh * TT:(dh + 1) * TT])
                                      for ic in range(4)])
                    q.run("v", lambda v: [v.tensor_copy(cm.ot[:, ic * TT:(ic + 1) * TT],
                                                        cm.wf[:, ic * TT:(ic + 1) * TT])
                                          for ic in range(4)])

                    def mmd2(t, ss=ss):
                        ins = None
                        for ic in range(4):
                            ins = t.matmul(cm.pC[:], hp[:, ic * 512 + ss * P:ic * 512 + (ss + 1) * P],
                                           cm.ot[:, ic * TT:(ic + 1) * TT],
                                           start=(ic == 0), stop=(ic == 3), skip_group_check=True)
                        return ins
                    q.run("t", mmd2)
                    q.run("v", lambda v: v.tensor_copy(gt[:, 0:TT], cm.pC[:]))
                    dma_step(q, "g", [(y_d[e * CAPL + ss * P:e * CAPL + (ss + 1) * P,
                                           dh * TT:(dh + 1) * TT], gt[:, 0:TT])])

        for jt in range(T // P):
            dma_step(q, "y", [(ekt[:], ek_h[jt * P:(jt + 1) * P, :]),
                              (poskt[:], posk_h[jt * P:(jt + 1) * P, :])])
            q.run("v", lambda v: v.memset(acc[:], 0.0))
            q.run("v", lambda v: v.tensor_scalar(scr8[:, 24:32], ekt[:], float(CAPL), None,
                                                 ALU.mult))
            q.run("v", lambda v: v.tensor_add(scr8[:, 24:32], scr8[:, 24:32], poskt[:]))
            for k8 in range(8):
                q.run("v", lambda v, k8=k8: v.tensor_tensor(
                    scr[:], iot[:], ekt[:, k8:k8 + 1].broadcast_to([P, 64]), ALU.is_equal))
                q.run("v", lambda v: v.tensor_mul(scr[:], scr[:], offt[:]))
                q.run("v", lambda v, k8=k8: v.reduce_sum(scr8[:, 8 + k8:9 + k8], scr[:], axis=AXX))
            q.run("v", lambda v: v.tensor_add(scr8[:, 16:24], poskt[:], scr8[:, 8:16]))
            q.run("v", lambda v: v.tensor_scalar(scr8[:, 16:24], scr8[:, 16:24], 1024.0, None,
                                                 ALU.is_lt))
            q.run("v", lambda v: v.tensor_scalar(scr8[:, 0:8], poskt[:], float(CAPL), None,
                                                 ALU.is_lt))
            q.run("v", lambda v: v.tensor_mul(scr8[:, 16:24], scr8[:, 16:24], scr8[:, 0:8]))
            for k8 in range(8):
                q.run("v", lambda v, k8=k8: [
                    v.tensor_mul(scr[:, 0:1], scr8[:, 24 + k8:25 + k8], scr8[:, 16 + k8:17 + k8]),
                    v.tensor_scalar(scr[:, 1:2], scr8[:, 16 + k8:17 + k8], -float(ZROW),
                                    float(ZROW), ALU.mult, ALU.add),
                    v.tensor_add(scr[:, 0:1], scr[:, 0:1], scr[:, 1:2])])
                q.run("v", lambda v: v.tensor_copy(idxs[:, 1:2], scr[:, 0:1]))
                q.run("g", lambda g: g.indirect_dma_start(
                    gt[:], None, y_d[:], bass.IndirectOffsetOnAxis(idxs[:, 1:2], 0)),
                    inc=16)
                q.run("v", lambda v, jt=jt, k8=k8: v.scalar_tensor_tensor(
                    acc[:], gt[:], wfin[:, jt * 8 + k8:jt * 8 + k8 + 1], acc[:],
                    ALU.mult, ALU.add))
            dma_step(q, "y", [(cm.xs[:, c * P:(c + 1) * P],
                               xa_h[c * P:(c + 1) * P, jt * P:(jt + 1) * P]) for c in range(KC)])
            for c in range(KC):
                q.run("t", lambda t, c=c: t.transpose(pT[:], acc[:, c * P:(c + 1) * P], id_s[:]))
                q.run("v", lambda v, c=c: v.tensor_add(cm.xs[:, c * P:(c + 1) * P],
                                                       cm.xs[:, c * P:(c + 1) * P], pT[:]))
            tstore = dma_step(q, "g", [(xout_h[c * P:(c + 1) * P, jt * P:(jt + 1) * P],
                                        cm.xs[:, c * P:(c + 1) * P]) for c in range(KC)])
            if debug and jt == 0:
                dma_step(q, "g", [(dacc_h[:], acc[:])])
        if debug:
            for r in range(8):
                dma_step(q, "y", [(cm.xbs[:, 0:D], buf_d[r * P:(r + 1) * P, :])])
                q.run("v", lambda v: v.tensor_copy(cm.xs[:, 0:D], cm.xbs[:, 0:D]))
                dma_step(q, "g", [(dbuf_h[r * P:(r + 1) * P, :], cm.xs[:, 0:D])])
                dma_step(q, "y", [(cm.xbs[:, 0:D], y_d[r * P:(r + 1) * P, :])])
                q.run("v", lambda v: v.tensor_copy(cm.xs[:, 0:D], cm.xbs[:, 0:D]))
                dma_step(q, "g", [(dy_h[r * P:(r + 1) * P, :], cm.xs[:, 0:D])])
            q.run("v", lambda v: v.tensor_copy(cm.sq[:, 0:128], wfin[:]))
            q.run("v", lambda v: v.tensor_copy(cm.xs[:, 0:128], cm.sq[:, 0:128]))
            tstore = dma_step(q, "g", [(dwf_h[:], cm.xs[:, 0:128])])
        q.run("g", lambda g: None, after=tstore)
    return nc


def prep_L2(inputs):
    out = {}
    out["wq"] = chunks8(np.asarray(inputs["L2_attn_w_q"], np.float32))
    out["wk"] = chunks8(np.asarray(inputs["L2_attn_w_k"], np.float32))
    out["wv"] = chunks8(np.asarray(inputs["L2_attn_w_v"], np.float32))
    out["wo"] = chunks8(np.asarray(inputs["L2_attn_w_o"], np.float32))
    out["bo"] = np.asarray(inputs["L2_attn_b_o"], np.float32).reshape(1, D)
    gamb = np.zeros((P, 16), np.float32)
    gamb[:, 0:8] = np.asarray(inputs["L2_attn_gamma"], np.float32).reshape(8, 128).T
    gamb[:, 8:16] = np.asarray(inputs["L2_attn_beta"], np.float32).reshape(8, 128).T
    out["gamb"] = gamb
    gq = np.asarray(inputs["L2_attn_g_q"], np.float32)
    gk = np.asarray(inputs["L2_attn_g_k"], np.float32)
    def gcol(vals):
        m = np.zeros((KC, P, 16), np.float32)
        for c in range(KC):
            for jl in range(2):
                m[c, jl * 64:(jl + 1) * 64, 2 * c + jl] = vals
        return np.ascontiguousarray(m.transpose(1, 0, 2).reshape(P, KC * 16))
    out["gqc"] = gcol(gq)
    out["gkc"] = gcol(gk)
    out["selred"] = gcol(np.ones(64, np.float32))
    selh = np.zeros((16, 16 * 64), np.float32)
    for h in range(16):
        selh[h, h * 64:(h + 1) * 64] = 1.0
    out["selh"] = selh
    out["gw"] = chunks8(np.asarray(inputs["L2_ffn_gate_w"], np.float32).T)
    out["gbias"] = np.asarray(inputs["L2_ffn_gate_bias"], np.float32).reshape(1, 64)
    ltri = np.tril(np.ones((P, P), np.float32), -1)
    out["ltri"] = np.ascontiguousarray(ltri.T)
    out["iota"] = np.tile(np.arange(64, dtype=np.float32)[None, :], (P, 1))
    out["idm"] = np.eye(P, dtype=np.float32)
    return out


def prep_L2b(inputs):
    out = {}
    out["wg"] = np.ascontiguousarray(np.asarray(inputs["L2_ffn_w_gate"], np.float32)
                                     .reshape(64, KC, P, 512))
    out["wu"] = np.ascontiguousarray(np.asarray(inputs["L2_ffn_w_up"], np.float32)
                                     .reshape(64, KC, P, 512))
    out["wd"] = np.ascontiguousarray(np.asarray(inputs["L2_ffn_w_down"], np.float32)
                                     .reshape(64, 4, P, D))
    out["iota"] = np.tile(np.arange(64, dtype=np.float32)[None, :], (P, 1))
    out["idm"] = np.eye(P, dtype=np.float32)
    return out


def run_L2(x_fm, inputs):
    w = prep_L2(inputs)
    nc = build_L2a()
    in_maps = [dict(w, xb=_roll_x(x_fm, c)) for c in range(NCORES)]
    res = run_bass_kernel_spmd(nc, in_maps, core_ids=list(range(NCORES))).results
    cnts = np.stack([res[c]["cnt"][0] for c in range(NCORES)])
    w2 = prep_L2b(inputs)
    nc2 = build_L2b()
    in_maps2 = []
    for c in range(NCORES):
        m = np.zeros((8, 1), np.float32)
        m[0:c] = 1.0
        in_maps2.append(dict(w2, xn=res[c]["xnout"], xa=res[c]["xaout"],
                             ek=res[c]["ek"], posk=res[c]["posk"], wk8=res[c]["wk8"],
                             cnts=cnts, mask=m))
    res2 = run_bass_kernel_spmd(nc2, in_maps2, core_ids=list(range(NCORES))).results
    x_new = np.empty_like(x_fm)
    _scatter_out(res2, x_new)
    return x_new


def _roll_x(x_fm, c):
    b, h = divmod(c, 2)
    xb = x_fm[b]
    if h == 0:
        return np.ascontiguousarray(xb)
    return np.ascontiguousarray(np.concatenate([xb[:, T:], xb[:, :T]], axis=1))


def _scatter_out(res, x_new, key="xout"):
    for c in range(NCORES):
        b, h = divmod(c, 2)
        x_new[b][:, h * T:(h + 1) * T] = res[c][key]


NCORES = 8


def run_L1(x_fm, inputs):
    w = prep_L1(inputs)
    nc = build_L1(0)
    in_maps = [dict(w, xb=_roll_x(x_fm, c)) for c in range(NCORES)]
    res = run_bass_kernel_spmd(nc, in_maps, core_ids=list(range(NCORES))).results
    x_new = np.empty_like(x_fm)
    _scatter_out(res, x_new)
    return x_new


def kernel(**inputs):
    x = np.asarray(inputs["L0_attn_x"], np.float32)
    x_fm = np.ascontiguousarray(x.transpose(0, 2, 1))
    x_fm = run_L0(x_fm, inputs)
    x_fm = run_L1(x_fm, inputs)
    x_fm = run_L2(x_fm, inputs)
    x_fm = run_L3(x_fm, inputs)
    return np.ascontiguousarray(x_fm.transpose(0, 2, 1))
